# version 07
# speedup vs baseline: 1.2449x; 1.0418x over previous
import contextlib
import numpy as np
import ml_dtypes
import concourse.bass as bass
import concourse.mybir as mybir
from concourse.bass_utils import run_bass_kernel_spmd

F32 = mybir.dt.float32
BF16 = mybir.dt.bfloat16
I32 = mybir.dt.int32
AF = mybir.ActivationFunctionType
ALU = mybir.AluOpType
AX = mybir.AxisListType
ENGINES = ("sync", "scalar", "vector", "tensor", "gpsimd")

class Builder:
    def __init__(self, name="prog"):
        self.nc = bass.Bass("TRN2", target_bir_lowering=False)
        self.es = contextlib.ExitStack()
        self.stages = []
        self._sb_bytes = 0
        self._finalized = False
        self._consts = {}

    def constf(self, value, dtype=None):
        dtype = dtype or F32
        key = (dtype, float(value))
        if key not in self._consts:
            t = self.sb(f"cst{len(self._consts)}", [128, 1], dtype)
            self.nc.const_aps.register(dtype, float(value), t[:, 0:1])
            self._consts[key] = t
        return self._consts[key][:, 0:1]

    def sb(self, name, shape, dtype, **kw):
        t = self.es.enter_context(self.nc.sbuf_tensor(name, list(shape), dtype, **kw))
        esize = {F32: 4, BF16: 2, I32: 4}[dtype]
        self._sb_bytes += int(np.prod(shape[1:])) * esize * 128
        return t

    def ps(self, name, shape, dtype):
        return self.es.enter_context(self.nc.psum_tensor(name, list(shape), dtype))

    def dram(self, name, shape, dtype, kind="Internal"):
        return self.nc.dram_tensor(name, list(shape), dtype, kind=kind).ap()

    def stage(self, **fns):
        for k in fns:
            assert k in ENGINES, k
        self.stages.append(fns)

    def finalize(self):
        assert not self._finalized
        self._finalized = True
        nstages = len(self.stages)
        ssem = {e: self.es.enter_context(self.nc.semaphore(f"st_{e}")) for e in ENGINES}
        dsem = {e: self.es.enter_context(self.nc.semaphore(f"dma_{e}"))
                for e in ("sync", "gpsimd", "scalar", "vector")}
        psem = {e: self.es.enter_context(self.nc.semaphore(f"pipe_{e}")) for e in ENGINES}
        def prepass():
            plan = []
            cnt = {("dma", e): 0 for e in dsem} | {("pipe", e): 0 for e in ENGINES}
            events = {}
            for si, st in enumerate(self.stages):
                stplan = {}
                for e in ENGINES:
                    fn = st.get(e)
                    if fn is not None:
                        w = EngineWrap(self, None, e, cnt, events, counting=True)
                        fn(w)
                    stplan[e] = (cnt.get(("dma", e)), cnt[("pipe", e)])
                plan.append(stplan)
            return plan, events
        plan, events = prepass()
        if self._consts:
            consts = dict(self._consts)
            def init_consts(w):
                for (dt, v), t in consts.items():
                    w.memset(t[:, 0:1], v)
            self.stages.insert(0, {"vector": init_consts})
            plan, events = prepass()
        self._events = events
        with self.es:
            with self.nc.Block() as block:
                for ename in ENGINES:
                    def make(ename):
                        def body(eng):
                            cnt2 = {("dma", e): 0 for e in dsem} | {("pipe", e): 0 for e in ENGINES}
                            for si, st in enumerate(self.stages):
                                if si > 0:
                                    for oe in ENGINES:
                                        eng.wait_ge(ssem[oe], si)
                                fn = st.get(ename)
                                w = None
                                if fn is not None:
                                    w = EngineWrap(self, eng, ename, cnt2, events,
                                                   counting=False, sems=(dsem, psem))
                                    fn(w)
                                if ename in dsem and plan[si][ename][0]:
                                    tgt = plan[si][ename][0]
                                    if tgt and tgt > (cnt2.get(("dmadone", ename)) or 0):
                                        eng.wait_ge(dsem[ename], tgt)
                                        cnt2[("dmadone", ename)] = tgt
                                li = getattr(w, "_last_inst", None) if w is not None else None
                                ptgt = plan[si][ename][1]
                                padv = ptgt > cnt2.get(("pipeself", ename), 0)
                                if ename == "tensor" and li is not None:
                                    li.then_inc(ssem[ename], 1)
                                    cnt2[("pipeself", ename)] = ptgt
                                else:
                                    if padv:
                                        eng.wait_ge(psem[ename], ptgt)
                                        cnt2[("pipeself", ename)] = ptgt
                                    eng.nop().then_inc(ssem[ename], 1)
                        return body
                    getattr(block, ename)(make(ename))
        return self.nc

class EngineWrap:
    COMPUTE = {"activation", "tensor_copy", "tensor_tensor", "tensor_add", "tensor_sub",
               "tensor_mul", "tensor_max", "tensor_scalar", "tensor_single_scalar",
               "tensor_scalar_add", "tensor_scalar_sub", "tensor_scalar_mul",
               "tensor_scalar_max", "tensor_scalar_min", "tensor_relu", "tensor_reduce",
               "reduce_max", "reduce_sum", "scalar_tensor_tensor", "tensor_tensor_scan",
               "memset", "memzero", "reciprocal", "select", "transpose", "bn_stats",
               "bn_aggr", "iota", "copy", "add", "mul", "stream_shuffle"}

    def __init__(self, b, eng, ename, cnt, events, counting, sems=None):
        self._b = b
        self._eng = eng
        self._en = ename
        self._cnt = cnt
        self._events = events
        self._counting = counting
        self._sems = sems
        self._last_inst = None

    def dma(self, out, in_, **kw):
        pc = self._cnt[("pipe", self._en)]
        self._cnt[("dma", self._en)] += 16
        if self._counting:
            return
        dsem, psem = self._sems
        if pc and pc > self._cnt.get(("pipewaited", self._en), 0):
            self._eng.wait_ge(psem[self._en], pc)
            self._cnt[("pipewaited", self._en)] = pc
        self._eng.dma_start(out=out, in_=in_, **kw).then_inc(dsem[self._en], 16)

    def dma_transpose(self, out, in_):
        self._cnt[("dma", self._en)] += 16
        if self._counting:
            return
        dsem, _ = self._sems
        self._eng.dma_start_transpose(out, in_).then_inc(dsem[self._en], 16)

    def wait_dma(self):
        tgt = self._cnt[("dma", self._en)]
        if self._counting:
            self._cnt[("dmadone", self._en)] = tgt
            return
        if tgt:
            dsem, _ = self._sems
            self._eng.wait_ge(dsem[self._en], tgt)
            self._cnt[("dmadone", self._en)] = tgt

    def ev(self, name):
        if self._en in ("tensor", "sync"):
            self._cnt[("pipe", self._en)] += 1
            tgt = self._cnt[("pipe", self._en)]
            if self._counting:
                self._events[name] = (self._en, tgt)
            else:
                _, psem = self._sems
                if self._en == "tensor" and self._last_inst is not None:
                    self._last_inst.then_inc(psem[self._en], 1)
                    self._last_inst = None
                else:
                    self._eng.nop().then_inc(psem[self._en], 1)
            return tgt
        tgt = self._cnt[("pipe", self._en)]
        if self._counting:
            self._events[name] = (self._en, tgt)
        return tgt

    def ev_dma(self, name):
        tgt = self._cnt[("dma", self._en)]
        if self._counting:
            self._events[name] = (("dma", self._en), tgt)
        return tgt

    def wait(self, name):
        if self._counting:
            return
        kind, tgt = self._events[name]
        dsem, psem = self._sems
        if isinstance(kind, tuple):
            self._eng.wait_ge(dsem[kind[1]], tgt)
        else:
            self._eng.wait_ge(psem[kind], tgt)

    def __getattr__(self, name):
        if name.startswith("_"):
            raise AttributeError(name)
        if name in self.COMPUTE and self._en != "tensor":
            def call(*a, **kw):
                self._cnt[("pipe", self._en)] += 1
                tgt = self._cnt[("pipe", self._en)]
                if self._counting:
                    if name == "activation":
                        for key in ("bias", "alpha"):
                            v = kw.get(key)
                            if isinstance(v, (int, float)) and v != 0:
                                self._b.constf(float(v))
                    return None
                _, psem = self._sems
                if tgt > 1:
                    self._eng.wait_ge(psem[self._en], tgt - 1)
                inst = getattr(self._eng, name)(*a, **kw)
                inst.then_inc(psem[self._en], 1)
                self._last_inst = inst
                return inst
            return call
        if self._counting:
            return lambda *a, **kw: _FakeInst()
        raw = getattr(self._eng, name)
        if name in ("matmul", "ldweights", "transpose", "matmul_mx"):
            def wrapped(*a, **kw):
                inst = raw(*a, **kw)
                self._last_inst = inst
                return inst
            return wrapped
        return raw

class _FakeInst:
    def then_inc(self, *a):
        return self

def to_bf(x):
    return np.ascontiguousarray(x).astype(ml_dtypes.bfloat16)

def to_f32(x):
    return np.ascontiguousarray(np.asarray(x, np.float32))

"""Shared emission helpers for the launch programs."""
import numpy as np


def ln_stage_stats(b, tiles, scr, junk, E, base=0):
    nt = len(tiles)

    def s1s(w):
        for i in range(nt):
            w.activation(out=junk[:, 0:E], in_=tiles[i][:, 0:E], func=AF.Square,
                         accum_out=scr[:, base+8*i+1:base+8*i+2])

    def s1v(w):
        for i in range(nt):
            w.reduce_sum(scr[:, base+8*i+0:base+8*i+1], tiles[i][:, 0:E], axis=AX.X)

    b.stage(scalar=s1s, vector=s1v)
    return nt

def ln_stage_cols(b, scr, nt, E, eps, base=0, mean_zero=False):
    def s2v(w):
        for i in range(nt):
            o = base + 8*i
            if not mean_zero:
                w.tensor_scalar(out=scr[:, o+2:o+3], in0=scr[:, o+0:o+1],
                                scalar1=1.0/E, scalar2=None, op0=ALU.mult)
            w.tensor_scalar(out=scr[:, o+3:o+4], in0=scr[:, o+1:o+2],
                            scalar1=1.0/E, scalar2=None, op0=ALU.mult)
            if mean_zero:
                w.tensor_scalar(out=scr[:, o+4:o+5], in0=scr[:, o+3:o+4],
                                scalar1=-1.0, scalar2=None, op0=ALU.mult)
            else:
                w.tensor_scalar(out=scr[:, o+4:o+5], in0=scr[:, o+2:o+3],
                                scalar1=scr[:, o+2:o+3], scalar2=scr[:, o+3:o+4],
                                op0=ALU.mult, op1=ALU.subtract)
    b.stage(vector=s2v)

    def s3s(w):
        for i in range(nt):
            o = base + 8*i
            w.activation(out=scr[:, o+5:o+6], in_=scr[:, o+4:o+5], func=AF.Sqrt,
                         scale=-1.0, bias=eps)
    b.stage(scalar=s3s)

def ln_tm(b, xtiles, xntiles, scr, junk, E, eps=1e-5, base=0):
    nt = ln_stage_stats(b, xtiles, scr, junk, E, base)
    ln_stage_cols(b, scr, nt, E, eps, base)

    def s4v(w):
        for i in range(nt):
            o = base + 8*i
            w.reciprocal(scr[:, o+6:o+7], scr[:, o+5:o+6])
        for i in range(nt):
            o = base + 8*i
            w.tensor_scalar(out=xntiles[i][:, 0:E], in0=xtiles[i][:, 0:E],
                            scalar1=scr[:, o+2:o+3], scalar2=scr[:, o+6:o+7],
                            op0=ALU.subtract, op1=ALU.mult)
    b.stage(vector=s4v)

def postln_add(b, xtiles, ytiles, scr, junk, tmp, E, eps=1e-5, base=0):
    nt = ln_stage_stats(b, ytiles, scr, junk, E, base)
    ln_stage_cols(b, scr, nt, E, eps, base)

    def s4v(w):
        for i in range(nt):
            o = base + 8*i
            w.reciprocal(scr[:, o+6:o+7], scr[:, o+5:o+6])
        for i in range(nt):
            o = base + 8*i
            w.tensor_scalar(out=tmp[:, 0:E], in0=ytiles[i][:, 0:E],
                            scalar1=scr[:, o+2:o+3], scalar2=scr[:, o+6:o+7],
                            op0=ALU.subtract, op1=ALU.mult)
            w.tensor_add(xtiles[i][:, 0:E], xtiles[i][:, 0:E], tmp[:, 0:E])
    b.stage(vector=s4v)


def transpose_grid(b, jobs, pbanks, ident, batch=8):
    nb = (len(jobs) + batch - 1) // batch
    for k in range(nb + 1):
        def st(w, k=k):
            if k < nb:
                pb = pbanks[k % 2]
                for q, (src, dst) in enumerate(jobs[k*batch:(k+1)*batch]):
                    w.transpose(pb[:, q*128:(q+1)*128], src, ident[:, 0:128])

        def sv(w, k=k):
            if k > 0:
                pb = pbanks[(k-1) % 2]
                for q, (src, dst) in enumerate(jobs[(k-1)*batch:k*batch]):
                    w.tensor_copy(dst, pb[:, q*128:(q+1)*128])
        b.stage(tensor=st if k < nb else None, vector=sv if k > 0 else None)

def transpose_fm_tm(b, src_tiles, dst_tiles, R, C, pbanks, ident):
    jobs = []
    for i in range(R):
        for j in range(C):
            jobs.append((src_tiles[i][:, j*128:(j+1)*128],
                         dst_tiles[j][:, i*128:(i+1)*128]))
    transpose_grid(b, jobs, pbanks, ident)


def mm_stream(b, wdram, K, M, X_tiles, T, psb, wbufs, evac, wdtype=F32,
              tsz=512, loader="gpsimd", evac_eng="vector", post=None):
    nk = K // 128
    nm = (M + 127) // 128
    ntc = (T + tsz - 1) // tsz
    if 2 * ntc > len(psb):
        assert ntc <= len(psb)
        return _mm_stream_seq(b, wdram, K, M, X_tiles, T, psb, wbufs, evac,
                              tsz, loader, evac_eng, post, nk, nm, ntc)

    def load_chunk(w, mi, buf):
        m0 = mi * 128
        mw = min(128, M - m0)
        nch = 4 if nk >= 8 else 1
        step = (nk + nch - 1) // nch
        for c0 in range(0, nk, step):
            c1 = min(nk, c0 + step)
            w.dma(out=buf[:, c0*mw:c1*mw].rearrange("p (c m) -> p c m", m=mw),
                  in_=wdram[:, m0:m0+mw].rearrange("(c p) m -> p c m", p=128)[:, c0:c1, :])

    b.stage(**{loader: lambda w: load_chunk(w, 0, wbufs[0])})
    for mi in range(nm + (2 if post else 1)):
        def st(w, mi=mi):
            if mi < nm:
                mw = min(128, M - mi*128)
                buf = wbufs[mi % 2]
                for ti in range(ntc):
                    tw = min(tsz, T - ti*tsz)
                    pp = psb[(mi % 2) * ntc + ti]
                    for kc in range(nk):
                        w.matmul(pp[0:mw, 0:tw], buf[:, kc*mw:(kc+1)*mw],
                                 X_tiles[kc][:, ti*tsz:ti*tsz+tw],
                                 start=(kc == 0), stop=(kc == nk - 1))

        def sl(w, mi=mi):
            if mi + 1 < nm:
                load_chunk(w, mi + 1, wbufs[(mi + 1) % 2])

        def se(w, mi=mi):
            if 0 < mi <= nm:
                for ti in range(ntc):
                    tw = min(tsz, T - (ti)*tsz)
                    evac(mi - 1, ti, psb[((mi - 1) % 2) * ntc + ti], w,
                         min(128, M - (mi-1)*128), tw)
        def sp(w, mi=mi):
            if post is not None and 1 < mi <= nm + 1:
                post(w, mi - 2)
        kw = {}
        if mi < nm:
            kw["tensor"] = st
            kw[loader] = sl
        if 0 < mi <= nm:
            kw[evac_eng] = se
        if post is not None and 1 < mi <= nm + 1:
            if "gpsimd" in kw:
                prev = kw["gpsimd"]
                def both(w, prev=prev, sp=sp):
                    prev(w)
                    sp(w)
                kw["gpsimd"] = both
            else:
                kw["gpsimd"] = sp
        b.stage(**kw)


def round_emit(w, dst, src, scrI, scrA, scrB, N, minus=1.0, clip1=False,
               post_scale=None):
    w.tensor_copy(scrI[:, 0:N], src)
    w.tensor_copy(scrA[:, 0:N], scrI[:, 0:N])
    w.tensor_sub(scrB[:, 0:N], src, scrA[:, 0:N])
    w.scalar_tensor_tensor(out=scrA[:, 0:N], in0=scrB[:, 0:N], scalar=0.5,
                           in1=scrA[:, 0:N], op0=ALU.is_gt, op1=ALU.add)
    w.scalar_tensor_tensor(out=scrA[:, 0:N], in0=scrB[:, 0:N], scalar=-0.5,
                           in1=scrA[:, 0:N], op0=ALU.is_ge, op1=ALU.add)
    if clip1:
        w.tensor_scalar(out=scrA[:, 0:N], in0=scrA[:, 0:N], scalar1=minus,
                        scalar2=-1.0, op0=ALU.subtract, op1=ALU.max)
        w.tensor_scalar(out=dst, in0=scrA[:, 0:N], scalar1=1.0, scalar2=None,
                        op0=ALU.min)
    elif post_scale is not None:
        w.tensor_scalar(out=dst, in0=scrA[:, 0:N], scalar1=minus,
                        scalar2=post_scale, op0=ALU.subtract, op1=ALU.mult)
    else:
        w.tensor_scalar(out=dst, in0=scrA[:, 0:N], scalar1=minus, scalar2=None,
                        op0=ALU.subtract)

KO_ln_fm_id = [0]


def ln_fm(b, xtiles, T, psf, srow, onesf, sq2, onesb=None, eps=1e-5,
          sq_row_tmp=None):
    nk = len(xtiles)
    nsl = (T + 511) // 512
    ob_ = onesb if onesb is not None else onesf
    def p1(w):
        for j in range(nsl):
            for kc in range(nk):
                w.matmul(psf[j][0:1, :], ob_[:, 0:1],
                         xtiles[kc][:, j*512:(j+1)*512], start=(kc == 0),
                         stop=(kc == nk - 1))
    b.stage(tensor=p1)
    def p1e(w):
        for j in range(nsl):
            w.tensor_copy(srow[0:1, j*512:(j+1)*512], psf[j][0:1, :])
    b.stage(vector=p1e)
    for k in range(nk + 1):
        def ps(w, k=k):
            if k < nk:
                w.activation(out=sq2[k % 2][:, 0:T], in_=xtiles[k][:, 0:T], func=AF.Square)
        def pt(w, k=k):
            if k > 0:
                for j in range(nsl):
                    w.matmul(psf[j][0:1, :], ob_[:, 0:1],
                             sq2[(k-1) % 2][:, j*512:(j+1)*512],
                             start=(k == 1), stop=(k == nk))
        b.stage(scalar=ps if k < nk else None, tensor=pt if k > 0 else None)
    def p2e(w):
        for j in range(nsl):
            w.tensor_copy(srow[0:1, T + j*512:T + (j+1)*512], psf[j][0:1, :])
    b.stage(vector=p2e)
    def p3(w):
        w.tensor_scalar(out=srow[0:1, 0:T], in0=srow[0:1, 0:T],
                        scalar1=1.0/2048.0, scalar2=None, op0=ALU.mult)
        w.tensor_scalar(out=srow[0:1, T:2*T], in0=srow[0:1, T:2*T],
                        scalar1=-1.0/2048.0, scalar2=None, op0=ALU.mult)
    b.stage(vector=p3)
    def p3b(w):
        for j in range(nsl):
            w.activation(out=psf[j][0:1, 0:512], in_=srow[0:1, j*512:(j+1)*512],
                         func=AF.Square)
    b.stage(scalar=p3b)
    def p3c(w):
        for j in range(nsl):
            w.tensor_add(srow[0:1, T + j*512:T + (j+1)*512],
                         srow[0:1, T + j*512:T + (j+1)*512], psf[j][0:1, 0:512])
    b.stage(vector=p3c)
    def p4(w):
        w.activation(out=srow[0:1, T:2*T], in_=srow[0:1, T:2*T], func=AF.Sqrt,
                     scale=-1.0, bias=eps)
    b.stage(scalar=p4)
    def p5(w):
        w.reciprocal(srow[0:1, T:2*T], srow[0:1, T:2*T])
    b.stage(vector=p5)
    for j in range(nsl + 1):
        def pb(w, j=j):
            if j < nsl:
                w.matmul(psf[(2*j) % 4][:, :], onesf[0:1, 0:128],
                         srow[0:1, j*512:(j+1)*512], start=True, stop=True)
                w.matmul(psf[(2*j+1) % 4][:, :], onesf[0:1, 0:128],
                         srow[0:1, T + j*512:T + (j+1)*512], start=True, stop=True)
        def pv(w, j=j):
            if j > 0:
                jj = j - 1
                for kc in range(nk):
                    w.tensor_sub(xtiles[kc][:, jj*512:(jj+1)*512],
                                 xtiles[kc][:, jj*512:(jj+1)*512],
                                 psf[(2*jj) % 4][:, :])
                    w.tensor_mul(xtiles[kc][:, jj*512:(jj+1)*512],
                                 xtiles[kc][:, jj*512:(jj+1)*512],
                                 psf[(2*jj+1) % 4][:, :])
        b.stage(tensor=pb if j < nsl else None, vector=pv if j > 0 else None)
    return None, None


def mm_tm(b, wdram, K, M, X_tiles, T, psf, wbufs, evac, evac_eng="vector",
          tag="mmtm"):
    nk = K // 128
    ntt = T // 128
    nmc = (M + 127) // 128

    def load_chunk(w, ci, buf):
        m0 = ci * 128
        mw = min(128, M - m0)
        for c0 in range(0, nk, 4):
            c1 = min(nk, c0 + 4)
            w.dma(out=buf[:, c0*mw:c1*mw].rearrange("p (c m) -> p c m", m=mw),
                  in_=wdram[:, m0:m0+mw].rearrange("(c p) m -> p c m", p=128)[:, c0:c1, :])
    b.stage(gpsimd=lambda w: load_chunk(w, 0, wbufs[0]))
    for ci in range(nmc):
        def st(w, ci=ci):
            mw = min(128, M - ci*128)
            buf = wbufs[ci % 2]
            for tt in range(ntt):
                if tt >= 4:
                    w.wait(f"{tag}e_{ci}_{tt-4}")
                pp = psf[tt % 4]
                for kc in range(nk):
                    w.matmul(pp[:, 0:mw], X_tiles[kc][:, tt*128:(tt+1)*128],
                             buf[:, kc*mw:(kc+1)*mw], start=(kc == 0),
                             stop=(kc == nk - 1))
                w.ev(f"{tag}m_{ci}_{tt}")
        def se(w, ci=ci):
            mw = min(128, M - ci*128)
            for tt in range(ntt):
                w.wait(f"{tag}m_{ci}_{tt}")
                evac(ci, tt, psf[tt % 4], w, mw)
                w.ev(f"{tag}e_{ci}_{tt}")
        def sldr(w, ci=ci):
            if ci + 1 < nmc:
                load_chunk(w, ci + 1, wbufs[(ci + 1) % 2])
        b.stage(tensor=st, gpsimd=sldr, **{evac_eng: se})

def ew_stream(b, indrams, outdram_list, nrow_tiles, ncols, bufs, obufs, fn,
              extra_eng=None):
    ni = len(indrams)
    for k in range(nrow_tiles + 2):
        def sload(w, k=k):
            if k < nrow_tiles:
                for i in range(ni):
                    w.dma(bufs[i][k % 2][:, 0:ncols],
                          indrams[i][k*128:(k+1)*128, 0:ncols])
        def scomp(w, k=k):
            if 0 < k <= nrow_tiles:
                fn(w, k - 1, [bufs[i][(k-1) % 2] for i in range(ni)],
                   [obufs[o][(k-1) % 2] for o in range(len(outdram_list))])
        def sstore(w, k=k):
            if 1 < k <= nrow_tiles + 1:
                for o, od in enumerate(outdram_list):
                    w.dma(od[(k-2)*128:(k-1)*128, 0:ncols], obufs[o][k % 2][:, 0:ncols])
        kw = {}
        if k < nrow_tiles:
            kw["sync"] = sload
        if 0 < k <= nrow_tiles:
            kw["vector"] = scomp
        if 1 < k <= nrow_tiles + 1:
            kw["gpsimd"] = sstore
        b.stage(**kw)


def mm_xd(b, wsl, K, M, xdram, T, psf, xring, wring, evac, tag,
          evac_eng="vector"):
    nk = K // 128
    nm = (M + 127) // 128
    ntc = (T + 511) // 512
    GG = 4
    ng = nk // GG
    assert len(xring) >= 8 and len(wring) >= 8
    for mi in range(nm + 1):
        def sld(w, mi=mi):
            if mi < nm:
                mw = min(128, M - mi*128)
                for g in range(ng):
                    if g >= 2:
                        w.wait(f"{tag}m_{mi}_{g-2}")
                    for kc in range(g*GG, (g+1)*GG):
                        w.dma(xring[kc % 8][:, 0:T], xdram[kc*128:(kc+1)*128, 0:T])
                        w.dma(wring[kc % 8][:, 0:mw], wsl(mi, kc))
                    w.wait_dma()
                    w.ev(f"{tag}l_{mi}_{g}")
        def st(w, mi=mi):
            if mi < nm:
                mw = min(128, M - mi*128)
                for g in range(ng):
                    w.wait(f"{tag}l_{mi}_{g}")
                    for kc in range(g*GG, (g+1)*GG):
                        for ti in range(ntc):
                            tw = min(512, T - ti*512)
                            pp = psf[(mi % 2) * ntc + ti]
                            w.matmul(pp[0:mw, 0:tw], wring[kc % 8][:, 0:mw],
                                     xring[kc % 8][:, ti*512:ti*512+tw],
                                     start=(kc == 0), stop=(kc == nk - 1))
                    w.ev(f"{tag}m_{mi}_{g}")
        def se(w, mi=mi):
            if mi > 0:
                for ti in range(ntc):
                    tw = min(512, T - ti*512)
                    evac(mi - 1, ti, psf[((mi - 1) % 2) * ntc + ti], w,
                         min(128, M - (mi-1)*128), tw)
        kw = {}
        if mi < nm:
            kw["tensor"] = st
            kw["sync"] = sld
        if mi > 0:
            kw[evac_eng] = se
        b.stage(**kw)

def mm_xk(b, wsl8, K, M, xdram, T, psf, xslots, wb2, evac, tag):
    nk = K // 128
    nm = M // 128
    NB = nk // 8
    assert T == 1024 and nm % 2 == 0 and len(xslots) >= 16
    for g in range(nm // 2):
        for band in range(NB + 1):
            def sload(w, g=g, band=band):
                for j in range(8):
                    kc = band*8 + j
                    w.dma(xslots[(band % 2)*8 + j][:, 0:T],
                          xdram[kc*128:(kc+1)*128, 0:T])
                w.wait_dma()
                w.ev(f"{tag}x_{g}_{band}")
            def wload(w, g=g, band=band):
                for ml in range(2):
                    w.dma(wb2[band % 2][:, ml*1024:(ml+1)*1024]
                          .rearrange("p (c m) -> p c m", m=128),
                          wsl8(2*g + ml, band)
                          .rearrange("(c p) m -> p c m", p=128))
                w.wait_dma()
                w.ev(f"{tag}w_{g}_{band}")
            def mmm(w, g=g, band=band):
                bb = band - 1
                w.wait(f"{tag}x_{g}_{bb}")
                w.wait(f"{tag}w_{g}_{bb}")
                for ml in range(2):
                    for j in range(8):
                        kc = bb*8 + j
                        for th in range(2):
                            w.matmul(
                                psf[2*ml + th][:, :],
                                wb2[bb % 2][:, ml*1024 + j*128:ml*1024 + (j+1)*128],
                                xslots[(bb % 2)*8 + j][:, th*512:(th+1)*512],
                                start=(kc == 0), stop=(kc == nk - 1))
            kw = {}
            if band < NB:
                kw["sync"] = sload
                kw["gpsimd"] = wload
            if band > 0:
                kw["tensor"] = mmm
            b.stage(**kw)
        def sev(w, g=g):
            for ml in range(2):
                mi = 2*g + ml
                for th in range(2):
                    evac(mi, th, psf[2*ml + th], w, min(128, M - mi*128), 512)
        b.stage(vector=sev)

def _mm_stream_seq(b, wdram, K, M, X_tiles, T, psb, wbufs, evac, tsz, loader,
                   evac_eng, post, nk, nm, ntc):
    def load_chunk(w, mi, buf):
        m0 = mi * 128
        mw = min(128, M - m0)
        for c0 in range(0, nk, 4):
            c1 = min(nk, c0 + 4)
            w.dma(out=buf[:, c0*mw:c1*mw].rearrange("p (c m) -> p c m", m=mw),
                  in_=wdram[:, m0:m0+mw].rearrange("(c p) m -> p c m", p=128)[:, c0:c1, :])
    b.stage(**{loader: lambda w: load_chunk(w, 0, wbufs[0])})
    for mi in range(nm):
        def st(w, mi=mi):
            mw = min(128, M - mi*128)
            buf = wbufs[mi % 2]
            for ti in range(ntc):
                tw = min(tsz, T - ti*tsz)
                for kc in range(nk):
                    w.matmul(psb[ti][0:mw, 0:tw], buf[:, kc*mw:(kc+1)*mw],
                             X_tiles[kc][:, ti*tsz:ti*tsz+tw],
                             start=(kc == 0), stop=(kc == nk - 1))
        def sl(w, mi=mi):
            if mi + 1 < nm:
                load_chunk(w, mi + 1, wbufs[(mi + 1) % 2])
        b.stage(tensor=st, **{loader: sl})
        def se(w, mi=mi):
            for ti in range(ntc):
                tw = min(tsz, T - ti*tsz)
                evac(mi, ti, psb[ti], w, min(128, M - mi*128), tw)
            if post is not None:
                w.ev(f"mmsq{id(wdram)}_{mi}")
        def sp(w, mi=mi):
            w.wait(f"mmsq{id(wdram)}_{mi}")
            post(w, mi)
        kw = {evac_eng: se}
        if post is not None:
            kw["gpsimd"] = sp
        b.stage(**kw)

def rms_fm(b, tiles, D, T, psf, srow, onesf, onesb, sqtmp, nw_col, eps=1e-5):
    nk = D // 128
    nsl = (T + 511) // 512
    for k in range(nk + 1):
        def ps_(w, k=k):
            if k < nk:
                w.activation(out=sqtmp[k % 2][:, 0:T], in_=tiles[k][:, 0:T],
                             func=AF.Square)
        def pt(w, k=k):
            if k > 0:
                for j in range(nsl):
                    w.matmul(psf[j][0:1, :], onesb[:, 0:1],
                             sqtmp[(k-1) % 2][:, j*512:(j+1)*512],
                             start=(k == 1), stop=(k == nk))
        b.stage(scalar=ps_ if k < nk else None, tensor=pt if k > 0 else None)
    def p2e(w):
        for j in range(nsl):
            w.tensor_copy(srow[0:1, j*512:(j+1)*512], psf[j][0:1, :])
    b.stage(vector=p2e)
    def p3(w):
        w.activation(out=srow[0:1, 0:T], in_=srow[0:1, 0:T], func=AF.Sqrt,
                     scale=1.0/D, bias=eps)
    b.stage(scalar=p3)
    def p4(w):
        w.reciprocal(srow[0:1, 0:T], srow[0:1, 0:T])
    b.stage(vector=p4)
    for j in range(nsl + 1):
        def pb(w, j=j):
            if j < nsl:
                w.matmul(psf[j % 4][:, :], onesf[0:1, 0:128],
                         srow[0:1, j*512:(j+1)*512], start=True, stop=True)
        def pv(w, j=j):
            if j > 0:
                jj = j - 1
                for kc in range(nk):
                    w.tensor_mul(tiles[kc][:, jj*512:(jj+1)*512],
                                 tiles[kc][:, jj*512:(jj+1)*512], psf[jj % 4][:, :])
                    w.tensor_scalar(out=tiles[kc][:, jj*512:(jj+1)*512],
                                    in0=tiles[kc][:, jj*512:(jj+1)*512],
                                    scalar1=nw_col[:, kc:kc+1], scalar2=None,
                                    op0=ALU.mult)
        b.stage(tensor=pb if j < nsl else None, vector=pv if j > 0 else None)

class _KOns:
    pass
KO = _KOns()

"""Launch builders. Each returns (nc, outs_spec). Host prep in prep_* functions."""
import numpy as np
import ml_dtypes

E = 2048
TT = 1024

def sl(t, i, w=1024):
    return t[:, i*w:(i+1)*w]


def build_B(Cin, ffn_mid, extra_inputs, proj_bias=False, rl_scale=False,
            pre_proj=None, need_round=True):
    b = Builder()
    nc = b.nc
    nkc = Cin // 128
    ins = {}
    ins["act"] = b.dram("act", [Cin, TT], BF16, kind="ExternalInput")
    ins["xin"] = b.dram("xin", [TT, E], F32, kind="ExternalInput")
    ins["wproj"] = b.dram("wproj", [Cin, E], F32, kind="ExternalInput")
    ins["identD"] = b.dram("identD", [128, 128], BF16, kind="ExternalInput")
    if proj_bias:
        ins["pbias"] = b.dram("pbias", [128, E], BF16, kind="ExternalInput")
    if rl_scale:
        ins["rl"] = b.dram("rl", [128, 8], F32, kind="ExternalInput")
    for nm, shp, dt in extra_inputs:
        ins[nm] = b.dram(nm, shp, dt, kind="ExternalInput")
    xout = b.dram("xout", [TT, E], F32, kind="ExternalOutput")

    x = [b.sb(f"x{i}", [128, E], F32) for i in range(8)]
    PA = [b.sb(f"pa{i}", [128, E], BF16) for i in range(16)]
    PB = [b.sb(f"pb{i}", [128, E], BF16) for i in range(8)]
    tmpA = b.sb("tmpA", [128, E], F32)
    tmpB = b.sb("tmpB", [128, E], F32)
    qI = b.sb("qI", [128, E], I32) if need_round else None
    scr = b.sb("scr", [128, 96], F32)
    wb = [b.sb(f"wb{i}", [128, (Cin // 128) * 128], BF16) for i in range(2)]
    ident = b.sb("ident", [128, 128], BF16)
    pbias_t = b.sb("pbias_t", [128, E], BF16) if proj_bias else None
    rl_t = b.sb("rl_t", [128, 8], F32) if rl_scale else None
    psf = [b.ps(f"psf{i}", [128, 512], F32) for i in range(4)]
    psb = [b.ps(f"psb{i}", [128, 1024], BF16) for i in range(2)]

    ctx = dict(b=b, ins=ins, x=x, PA=PA, PB=PB, tmpA=tmpA, tmpB=tmpB, qI=qI,
               scr=scr, wb=wb, ident=ident, psf=psf, psb=psb, junk=tmpA)

    actt = [sl(PA[i // 2], i % 2) for i in range(nkc)]

    def pA(w):
        w.dma(ident[:], ins["identD"])
        for i in range(nkc):
            w.dma(actt[i], ins["act"][i*128:(i+1)*128, :])
        if proj_bias:
            w.dma(pbias_t[:], ins["pbias"])
        if rl_scale:
            w.dma(rl_t[:], ins["rl"])
    b.stage(sync=pA)

    ctx["actt"] = actt
    if pre_proj is not None:
        pre_proj(b, ctx)
    ao_fm = [sl(PB[i // 2], i % 2) for i in range(16)]
    def evB(mi, ti, pp, w, mw, tw):
        w.tensor_copy(ao_fm[mi][:, ti*512:ti*512+tw], pp[0:mw, 0:tw])
    KO.mm_stream(b, ins["wproj"], Cin, E, actt, TT, psf, wb, evB)

    ao_tm = PA[0:8]
    KO.transpose_fm_tm(b, ao_fm, ao_tm, 16, 8, psb, ident)

    def pC2(w):
        for i in range(8):
            w.dma(x[i][:], ins["xin"][i*128:(i+1)*128, :])
    b.stage(sync=pC2)
    if rl_scale or proj_bias:
        def pC3(w):
            for i in range(8):
                if rl_scale and proj_bias:
                    w.scalar_tensor_tensor(out=ao_tm[i][:], in0=ao_tm[i][:],
                                           scalar=rl_t[:, i:i+1], in1=pbias_t[:],
                                           op0=ALU.mult, op1=ALU.add)
                elif rl_scale:
                    w.tensor_scalar(out=ao_tm[i][:], in0=ao_tm[i][:],
                                    scalar1=rl_t[:, i:i+1], scalar2=None, op0=ALU.mult)
                else:
                    w.tensor_add(ao_tm[i][:], ao_tm[i][:], pbias_t[:])
        b.stage(vector=pC3)

    KO.postln_add(b, x, ao_tm, scr, tmpA, tmpB, E)

    import os
    if os.environ.get("BDBG"):
        dbg1 = b.dram("dbg1", [TT, E], F32, kind="ExternalOutput")
        def pD1(w):
            for i in range(8):
                w.dma(dbg1[i*128:(i+1)*128, :], x[i][:])
            w.wait_dma()
        b.stage(gpsimd=pD1)

    ffn_mid(b, ctx)

    KO.postln_add(b, x, ctx["ftiles"], scr, tmpA, tmpB, E)
    def pZ(w):
        for i in range(8):
            w.dma(xout[i*128:(i+1)*128, :], x[i][:])
        w.wait_dma()
    b.stage(gpsimd=pZ)
    return b


def ffn_mid_l0(b, ctx):
    x, PA, PB = ctx["x"], ctx["PA"], ctx["PB"]
    scr, tmpA, tmpB, qI = ctx["scr"], ctx["tmpA"], ctx["tmpB"], ctx["qI"]
    ins = ctx["ins"]
    KO.ln_stage_stats(b, x, scr, ctx["junk"], E)
    KO.ln_stage_cols(b, scr, 8, E, 1e-5)
    xq = PA[8:16]
    for i in range(8):
        o = 8 * i
        def s1(w, i=i, o=o):
            w.reciprocal(scr[:, o+6:o+7], scr[:, o+5:o+6])
            w.tensor_scalar(out=tmpB[:], in0=x[i][:], scalar1=scr[:, o+2:o+3],
                            scalar2=scr[:, o+6:o+7], op0=ALU.subtract, op1=ALU.mult)
            w.reduce_max(scr[:, 64+2*i:64+2*i+1], tmpB[:], axis=AX.X,
                         apply_absolute_value=True)
        b.stage(vector=s1)
        def s2(w, i=i, o=o):
            w.activation(out=tmpA[:], in_=tmpB[:], func=AF.Square,
                         accum_out=scr[:, 64+2*i+1:64+2*i+2])
        b.stage(scalar=s2)
        def s3s(w, i=i, o=o):
            w.activation(out=scr[:, o+5:o+6], in_=scr[:, 64+2*i+1:64+2*i+2],
                         func=AF.Sqrt, scale=1.0/E, bias=1e-8)
        b.stage(scalar=s3s)
        def s4(w, i=i, o=o):
            w.reciprocal(scr[:, o+0:o+1], scr[:, o+5:o+6])
            w.tensor_scalar(out=scr[:, o+1:o+2], in0=scr[:, 64+2*i:64+2*i+1],
                            scalar1=scr[:, o+0:o+1], scalar2=1.0/127.0,
                            op0=ALU.mult, op1=ALU.mult)
            w.tensor_scalar(out=scr[:, o+4:o+5], in0=scr[:, o+1:o+2],
                            scalar1=1e-8, scalar2=None, op0=ALU.add)
            w.reciprocal(scr[:, o+3:o+4], scr[:, o+4:o+5])
            w.tensor_scalar(out=tmpB[:], in0=tmpB[:], scalar1=scr[:, o+0:o+1],
                            scalar2=scr[:, o+3:o+4], op0=ALU.mult, op1=ALU.mult)
            KO.round_emit(w, xq[i][:], tmpB[:], qI, tmpA, tmpB, E)
        b.stage(vector=s4)
    xq_fm = [sl(ctx["PB"][i // 2], i % 2) for i in range(16)]
    KO.transpose_fm_tm(b, xq, xq_fm, 8, 16, ctx["psb"], ctx["ident"])
    gam_t = b.sb("gam_t", [128, 16], F32)
    biasb = b.sb("biasb", [128, E], BF16)
    def pL(w):
        w.dma(gam_t[:], ins["gam"])
        w.dma(biasb[:], ins["biasbc"])
    b.stage(sync=pL)
    f_fm = [sl(PA[i // 2], i % 2) for i in range(16)]
    def evF(mi, ti, pp, w, mw, tw):
        w.tensor_scalar(out=f_fm[mi][:, ti*512:ti*512+tw], in0=pp[0:mw, 0:tw],
                        scalar1=gam_t[:, mi:mi+1], scalar2=None, op0=ALU.mult)
    KO.mm_stream(b, ins["wq0T"], E, E, xq_fm, TT, ctx["psf"], ctx["wb"], evF)
    ftiles = PA[8:16]
    jobs = []
    for i in range(16):
        for j in range(8):
            jobs.append((f_fm[i][:, j*128:(j+1)*128], (i, j)))
    nb = (len(jobs) + 7) // 8
    for k in range(nb + 1):
        def st(w, k=k):
            if k < nb:
                pb = ctx["psb"][k % 2]
                for q, (src, dst) in enumerate(jobs[k*8:(k+1)*8]):
                    w.transpose(pb[:, q*128:(q+1)*128], src, ctx["ident"][:, 0:128])
        def sv(w, k=k):
            if k > 0:
                pb = ctx["psb"][(k-1) % 2]
                for q, (src, (i, j)) in enumerate(jobs[(k-1)*8:k*8]):
                    w.scalar_tensor_tensor(out=ftiles[j][:, i*128:(i+1)*128],
                                           in0=pb[:, q*128:(q+1)*128],
                                           scalar=scr[:, 8*j+1:8*j+2],
                                           in1=biasb[:, i*128:(i+1)*128],
                                           op0=ALU.mult, op1=ALU.add)
        b.stage(tensor=st if k < nb else None, vector=sv if k > 0 else None)
    ctx["ftiles"] = ftiles

def build_l0b():
    extra = [("wq0T", [E, E], BF16), ("gam", [128, 16], F32),
             ("biasbc", [128, E], BF16)]
    return build_B(4096, ffn_mid_l0, extra)

def prep_l0b(core, inp, T=None, gated=None, wq0=None, gam=None):
    bb, j = core // 2, core % 2
    tsl = slice(j * 1024, (j + 1) * 1024)
    gated = T["l0_gated"][bb, tsl].T if gated is None else gated[bb][tsl].T
    wq0 = T["l0_wq"] if wq0 is None else wq0
    gam = (T["l0_gamma"] if gam is None else np.asarray(gam, np.float32)) \
        .reshape(16, 128).T
    d = {
        "act": to_bf(gated),
        "xin": to_f32(inp["L0_attn_hidden_states"][bb, tsl]),
        "wproj": to_f32(inp["L0_attn_w_out"]),
        "identD": np.eye(128, dtype=ml_dtypes.bfloat16),
        "wq0T": to_bf(wq0.T),
        "gam": to_f32(gam),
        "biasbc": to_bf(np.broadcast_to(inp["L0_ffn_bias"], (128, E))),
    }
    return d


def build_q():
    b = Builder()
    specs = [
        ("l0", [256, E], "l0"),
        ("l1w1", [1024, E], "meaneps"), ("l1w2", [256, 8192], "meaneps"),
        ("l2w1", [1024, E], "sign"), ("l2w2", [256, 8192], "sign"),
        ("l3w1", [1024, E], "absmax"), ("l3w2", [256, 8192], "absmax"),
    ]
    ins, outs = {}, {}
    for nm, shp, mode in specs:
        ins[nm] = b.dram(nm, shp, F32, kind="ExternalInput")
        outs[nm] = b.dram(nm + "_q", shp, BF16, kind="ExternalOutput")
        if mode == "l0":
            outs[nm + "_g"] = b.dram(nm + "_g", [shp[0]], F32, kind="ExternalOutput")
    for nm in ("s_l1w1", "s_l1w2", "s_l3w1", "s_l3w2", "s_l2w1", "s_l2w2"):
        ins[nm] = b.dram(nm, [1, 8], F32, kind="ExternalInput")
    outs["am3"] = b.dram("am3", [1, 2], F32, kind="ExternalOutput")

    wt = [b.sb(f"wt{i}", [128, 8192], F32) for i in range(2)]
    ab = b.sb("ab", [128, 8192], F32)
    qI = b.sb("qI", [128, 8192], I32)
    sA = b.sb("sA", [128, 8192], F32)
    wq = [b.sb(f"wq{i}", [128, 8192], BF16) for i in range(2)]
    scr = b.sb("scrq", [128, 16], F32)
    srow = b.sb("srow", [1, 16], F32)

    stat_in = b.sb("stat_in", [1, 48], F32)
    def s0b(w):
        for i, nm in enumerate(["s_l1w1", "s_l1w2", "s_l2w1", "s_l2w2", "s_l3w1", "s_l3w2"]):
            w.dma(stat_in[0:1, 8*i:8*i+8], ins[nm])
    b.stage(sync=s0b)
    nels = {"l1w1": 8192.0*E, "l1w2": 8192.0*E, "l2w1": 8192.0*E, "l2w2": 8192.0*E}
    def s1(w):
        w.memset(srow[0:1, 0:16], 0.0)
        for i, nm in enumerate(["l1w1", "l1w2", "l2w1", "l2w2"]):
            w.reduce_sum(srow[0:1, i:i+1], stat_in[0:1, 8*i:8*i+8], axis=AX.X)
            w.tensor_scalar(out=srow[0:1, i:i+1], in0=srow[0:1, i:i+1],
                            scalar1=1.0/nels[nm], scalar2=None, op0=ALU.mult)
        for i, nm in enumerate(["l3w1", "l3w2"]):
            w.reduce_max(srow[0:1, 8+i:9+i], stat_in[0:1, 32+8*i:40+8*i], axis=AX.X)
        w.tensor_scalar(out=srow[0:1, 0:2], in0=srow[0:1, 0:2], scalar1=1e-8,
                        scalar2=None, op0=ALU.max)
        w.reciprocal(srow[0:1, 4:6], srow[0:1, 0:2])
        w.tensor_scalar(out=srow[0:1, 6:8], in0=srow[0:1, 2:4], scalar1=0.7,
                        scalar2=None, op0=ALU.mult)
        w.reciprocal(srow[0:1, 10:12], srow[0:1, 8:10])
        w.tensor_scalar(out=srow[0:1, 12:14], in0=srow[0:1, 10:12], scalar1=127.0,
                        scalar2=None, op0=ALU.mult)
    b.stage(vector=s1)
    def s1b(w):
        w.dma(outs["am3"], srow[0:1, 8:10])
    b.stage(gpsimd=s1b)
    bcol = b.sb("bcol", [128, 16], F32)
    onesb = b.sb("onesb", [1, 128], F32)
    rowb = b.sb("rowb", [1, 16], F32)
    psq = b.ps("psq", [128, 16], F32)
    def s3b(w):
        w.memset(onesb[:], 1.0)
        w.tensor_copy(rowb[0:1, :], srow[0:1, 0:16])
    b.stage(vector=s3b)
    def s3c(w):
        w.matmul(psq[:], onesb[0:1, 0:128], rowb[0:1, :], start=True, stop=True)
        w.ev("bcq")
    def s3d(w):
        w.wait("bcq")
        w.tensor_copy(bcol[:, 0:16], psq[:])
    b.stage(tensor=s3c, vector=s3d)

    jobs = []
    for nm, shp, mode in specs:
        ntile = (shp[0] + 127) // 128
        for i in range(ntile):
            jobs.append((nm, shp, mode, i))
    for k in range(len(jobs) + 2):
        def sload(w, k=k):
            if k < len(jobs):
                nm, shp, mode, i = jobs[k]
                rows = min(128, shp[0] - i*128)
                w.dma(wt[k % 2][0:rows, 0:shp[1]], ins[nm][i*128:i*128+rows, :])
        def scomp(w, k=k):
            if 0 < k <= len(jobs):
                nm, shp, mode, i = jobs[k-1]
                rows = min(128, shp[0] - i*128)
                N = shp[1]
                src = wt[(k-1) % 2][0:rows, 0:N]
                dst = wq[(k-1) % 2][0:rows, 0:N]
                if mode == "l0":
                    w.scalar_tensor_tensor(out=ab[0:rows, 0:N], in0=src, scalar=-1.0,
                                           in1=src, op0=ALU.mult, op1=ALU.max)
                    w.reduce_sum(scr[0:rows, 0:1], ab[0:rows, 0:N], axis=AX.X)
                    w.tensor_scalar(out=scr[0:rows, 1:2], in0=scr[0:rows, 0:1],
                                    scalar1=1.0/N, scalar2=None, op0=ALU.mult)
                    w.tensor_scalar(out=scr[0:rows, 2:3], in0=scr[0:rows, 1:2],
                                    scalar1=1e-8, scalar2=None, op0=ALU.add)
                    w.reciprocal(scr[0:rows, 3:4], scr[0:rows, 2:3])
                    w.tensor_scalar(out=ab[0:rows, 0:N], in0=src,
                                    scalar1=scr[0:rows, 3:4], scalar2=None, op0=ALU.mult)
                    KO.round_emit(w, dst, ab[0:rows, 0:N], qI, sA, ab, N, clip1=True)
                elif mode == "meaneps":
                    w.tensor_scalar(out=ab[0:rows, 0:N], in0=src,
                                    scalar1=bcol[0:rows, 4:5] if nm == "l1w1" else bcol[0:rows, 5:6],
                                    scalar2=None, op0=ALU.mult)
                    KO.round_emit(w, dst, ab[0:rows, 0:N], qI, sA, ab, N, clip1=True)
                elif mode == "sign":
                    thr = bcol[0:rows, 6:7] if nm == "l2w1" else bcol[0:rows, 7:8]
                    w.scalar_tensor_tensor(out=ab[0:rows, 0:N], in0=src, scalar=-1.0,
                                           in1=src, op0=ALU.mult, op1=ALU.max)
                    w.scalar_tensor_tensor(out=sA[0:rows, 0:N], in0=ab[0:rows, 0:N],
                                           scalar=thr, in1=src, op0=ALU.is_ge,
                                           op1=ALU.mult)
                    w.ev(f"q_sgn{k}")
                else:
                    sc = bcol[0:rows, 12:13] if nm == "l3w1" else bcol[0:rows, 13:14]
                    w.tensor_scalar(out=ab[0:rows, 0:N], in0=src, scalar1=sc,
                                    scalar2=None, op0=ALU.mult)
                    KO.round_emit(w, dst, ab[0:rows, 0:N], qI, sA, ab, N)
        def ssgn(w, k=k):
            if 0 < k <= len(jobs):
                nm, shp, mode, i = jobs[k-1]
                if mode == "sign":
                    rows = min(128, shp[0] - i*128)
                    N = shp[1]
                    w.wait(f"q_sgn{k}")
                    w.activation(out=wq[(k-1) % 2][0:rows, 0:N], in_=sA[0:rows, 0:N],
                                 func=AF.Sign)
        def sstore(w, k=k):
            if 1 < k <= len(jobs) + 1:
                nm, shp, mode, i = jobs[k-2]
                rows = min(128, shp[0] - i*128)
                w.dma(outs[nm][i*128:i*128+rows, :], wq[k % 2][0:rows, 0:shp[1]])
        kw = {}
        if k < len(jobs):
            kw["sync"] = sload
        if 0 < k <= len(jobs):
            kw["vector"] = scomp
            kw["scalar"] = ssgn
        if 1 < k <= len(jobs) + 1:
            kw["gpsimd"] = sstore
        b.stage(**kw)
    g0 = b.sb("g0", [128, 2], F32)
    def gload(w):
        for i in range(2):
            w.dma(wt[i][:, 0:E], ins["l0"][i*128:(i+1)*128, :])
    b.stage(sync=gload)
    def gcomp(w):
        for i in range(2):
            w.scalar_tensor_tensor(out=ab[:, 0:E], in0=wt[i][:, 0:E], scalar=-1.0,
                                   in1=wt[i][:, 0:E], op0=ALU.mult, op1=ALU.max)
            w.reduce_sum(g0[:, i:i+1], ab[:, 0:E], axis=AX.X)
            w.tensor_scalar(out=g0[:, i:i+1], in0=g0[:, i:i+1],
                            scalar1=1.0/E, scalar2=None, op0=ALU.mult)
    b.stage(vector=gcomp)
    def gstore(w):
        for i in range(2):
            w.dma(outs["l0_g"][i*128:(i+1)*128], g0[:, i:i+1])
        w.wait_dma()
    b.stage(gpsimd=gstore)
    return b

def prep_q(core, inp, T=None):
    r0 = core
    def shard(wname, rows):
        wfull = np.asarray(inp[wname], np.float32)
        per = wfull.shape[0] // 8
        return to_f32(wfull[r0*per:(r0+1)*per])
    d = {
        "l0": shard("L0_ffn_weight", 256),
        "l1w1": shard("L1_ffn_w1", 1024), "l1w2": shard("L1_ffn_w2", 256),
        "l2w1": shard("L2_ffn_w1", 1024), "l2w2": shard("L2_ffn_w2", 256),
        "l3w1": shard("L3_ffn_W1", 1024), "l3w2": shard("L3_ffn_W2", 256),
    }
    for snm, wname in [("s_l1w1", "L1_ffn_w1"), ("s_l1w2", "L1_ffn_w2"),
                       ("s_l2w1", "L2_ffn_w1"), ("s_l2w2", "L2_ffn_w2")]:
        wfull = np.asarray(inp[wname], np.float32)
        per = wfull.shape[0] // 8
        sums = np.array([[np.abs(wfull[i*per:(i+1)*per]).sum(dtype=np.float64)
                          for i in range(8)]], np.float32)
        d[snm] = sums
    for snm, wname in [("s_l3w1", "L3_ffn_W1"), ("s_l3w2", "L3_ffn_W2")]:
        wfull = np.asarray(inp[wname], np.float32)
        per = wfull.shape[0] // 8
        d[snm] = np.array([[np.abs(wfull[i*per:(i+1)*per]).max()
                            for i in range(8)]], np.float32)
    return d


def build_l1a():
    b = Builder()
    ins = {}
    for nm, shp in [("xT", [E, E]), ("wq1", [E, 128]), ("wq2s", [128, E]),
                    ("wk1", [E, 128]), ("wk2s", [128, E]), ("wvs", [E, E])]:
        ins[nm] = b.dram(nm, shp, F32, kind="ExternalInput")
    o_out = b.dram("o_out", [E, E], BF16, kind="ExternalOutput")
    ssq_out = b.dram("ssq_out", [1, E], F32, kind="ExternalOutput")
    q_d = b.dram("q_d", [E, E], BF16)
    a_d = b.dram("a_d", [E, E], F32)
    u_d = b.dram("u_d", [E, E], F32)
    v_d = b.dram("v_d", [E, E], BF16)

    xn = [b.sb(f"xn{i}", [128, E], BF16) for i in range(16)]
    sq2 = [b.sb(f"sq2{i}", [128, E], BF16) for i in range(2)]
    srow = b.sb("srow", [1, 4 * E], F32)
    onesf = b.sb("onesf", [1, 128], F32)
    wb = [b.sb(f"wb{i}", [128, 16 * 128], BF16) for i in range(2)]
    q1 = b.sb("q1", [128, E], BF16)
    onesbf = b.sb("onesbf", [128, 1], BF16)
    psf = [b.ps(f"psf{i}", [128, 512], F32) for i in range(4)]

    def pA(w):
        for i in range(16):
            w.dma(xn[i][:], ins["xT"][i*128:(i+1)*128, :])
    b.stage(gpsimd=pA)
    def pA2(w):
        w.memset(onesf[0:1, :], 1.0)
        w.memset(onesbf[:, 0:1], 1.0)
    b.stage(vector=pA2)
    KO.ln_fm(b, xn, E, psf, srow, onesf, sq2, onesb=onesbf)

    def ev1(mi, ti, pp, w, mw, tw):
        w.tensor_copy(q1[:, ti*512:ti*512+tw], pp[0:mw, 0:tw])
    KO.mm_stream(b, ins["wq1"], E, 128, xn, E, psf, wb, ev1)
    qh = [b.sb("qh0", [128, E], BF16)]
    def ev2(mi, ti, pp, w, mw, tw):
        w.activation(out=qh[0][:, ti*512:ti*512+tw], in_=pp[0:mw, 0:tw],
                     func=AF.Silu)
    def post2(w, mi):
        w.dma(q_d[mi*128:(mi+1)*128, :], qh[0][:, 0:E])
    KO.mm_stream(b, ins["wq2s"], 128, E, [q1], E, psf, wb, ev2,
                 evac_eng="scalar", post=post2)

    KO.mm_stream(b, ins["wk1"], E, 128, xn, E, psf, wb, ev1)
    ah = [b.sb("ah0", [128, E], F32)]
    def ev3(mi, ti, pp, w, mw, tw):
        w.activation(out=ah[0][:, ti*512:ti*512+tw], in_=pp[0:mw, 0:tw],
                     func=AF.Sigmoid)
        w.activation(out=ah[0][:, ti*512:ti*512+tw],
                     in_=ah[0][:, ti*512:ti*512+tw], func=AF.Sigmoid)
    def post3(w, mi):
        w.dma(a_d[mi*128:(mi+1)*128, :], ah[0][:, 0:E])
    KO.mm_stream(b, ins["wk2s"], 128, E, [q1], E, psf, wb, ev3,
                 evac_eng="scalar", post=post3)

    vh = [b.sb("vh0", [128, E], BF16)]
    def ev4(mi, ti, pp, w, mw, tw):
        w.tensor_copy(vh[0][:, ti*512:ti*512+tw], pp[0:mw, 0:tw])
    def post4(w, mi):
        w.dma(v_d[mi*128:(mi+1)*128, :], vh[0][:, 0:E])
    KO.mm_stream(b, ins["wvs"], E, E, xn, E, psf, wb, ev4, post=post4)

    ub = [[b.sb(f"ub{o}{i}", [128, E], F32) for i in range(2)] for o in range(1)]
    ab2 = [b.sb(f"ab2{i}", [128, E], F32) for i in range(2)]
    vb2 = [b.sb(f"vb2{i}", [128, E], BF16) for i in range(2)]
    def ufn(w, k, tins, touts):
        a_t, v_t = tins
        w.tensor_scalar(out=touts[0][:, 0:E], in0=a_t[:, 0:E], scalar1=-1.0,
                        scalar2=1.0, op0=ALU.mult, op1=ALU.add)
        w.tensor_mul(touts[0][:, 0:E], touts[0][:, 0:E], v_t[:, 0:E])
    KO.ew_stream(b, [a_d, v_d], [u_d], 16, E, [ab2, vb2], ub, ufn)

    hb = b.sb("hb", [128, E], F32)
    ob = [sq2[1], vh[0]]
    o2 = sq2[0]
    qb2 = [b.sb(f"qb2{i}", [128, E], BF16) for i in range(2)]
    nsl = E // 512
    for k in range(16 + 2):
        def sload(w, k=k):
            if k < 16:
                w.dma(ab2[k % 2][:, 0:E], a_d[k*128:(k+1)*128, :])
                w.dma(ub[0][k % 2][:, 0:E], u_d[k*128:(k+1)*128, :])
                w.dma(qb2[k % 2][:, 0:E], q_d[k*128:(k+1)*128, :])
        def scomp(w, k=k):
            if 0 < k <= 16:
                kk = k - 1
                w.tensor_tensor_scan(out=hb[:, 0:E], data0=ab2[kk % 2][:, 0:E],
                                     data1=ub[0][kk % 2][:, 0:E], initial=0.0,
                                     op0=ALU.mult, op1=ALU.add)
                w.tensor_mul(ob[kk % 2][:, 0:E], qb2[kk % 2][:, 0:E], hb[:, 0:E])
                w.ev(f"l1o_{kk}")
        def ssq(w, k=k):
            if 0 < k <= 16:
                kk = k - 1
                w.wait(f"l1o_{kk}")
                w.activation(out=o2[:, 0:E], in_=ob[kk % 2][:, 0:E], func=AF.Square)
                w.ev(f"l1sq_{kk}")
        def smm(w, k=k):
            if 0 < k <= 16:
                kk = k - 1
                w.wait(f"l1sq_{kk}")
                for j in range(nsl):
                    w.matmul(psf[j][0:1, :], onesbf[:, 0:1],
                             o2[:, j*512:(j+1)*512], start=(kk == 0), stop=(kk == 15))
        def sstore(w, k=k):
            if 1 < k <= 17:
                w.dma(o_out[(k-2)*128:(k-1)*128, :], ob[k % 2][:, 0:E])
        kw = {}
        if k < 16:
            kw["sync"] = sload
        if 0 < k <= 16:
            kw["vector"] = scomp
            kw["scalar"] = ssq
            kw["tensor"] = smm
        if 1 < k <= 17:
            kw["gpsimd"] = sstore
        b.stage(**kw)
    def sev(w):
        for j in range(nsl):
            w.tensor_copy(srow[0:1, j*512:(j+1)*512], psf[j][0:1, :])
    b.stage(vector=sev)
    def sst(w):
        w.dma(ssq_out, srow[0:1, 0:E])
        w.wait_dma()
    b.stage(gpsimd=sst)
    return b

def prep_l1a(core, inp, T=None, x2=None):
    bb, j = core // 2, core % 2
    cs = slice(j * 2048, (j + 1) * 2048)
    xv = T["x2"] if x2 is None else x2
    return {
        "xT": to_f32(np.asarray(xv[bb]).T),
        "wq1": to_f32(inp["L1_attn_Wq1"]),
        "wq2s": to_f32(inp["L1_attn_Wq2"][:, cs]),
        "wk1": to_f32(inp["L1_attn_Wk1"]),
        "wk2s": to_f32(inp["L1_attn_Wk2"][:, cs]),
        "wvs": to_f32(inp["L1_attn_Wv"][:, cs]),
    }


def make_ffn_mid_l12(wk1, wk2, bk1, bk2):
    def mid(b, ctx):
        x, PA, PB = ctx["x"], ctx["PA"], ctx["PB"]
        scr, tmpA, tmpB = ctx["scr"], ctx["tmpA"], ctx["tmpB"]
        ins = ctx["ins"]
        xnt = PA[8:16]
        KO.ln_tm(b, x, xnt, scr, ctx["junk"], E)
        xn_fm = [sl(PB[i // 2], i % 2) for i in range(16)]
        KO.transpose_fm_tm(b, xnt, xn_fm, 8, 16, ctx["psb"], ctx["ident"])
        b1t = b.sb("b1t", [128, 64], F32)
        b2t = b.sb("b2t", [128, 16], F32)
        def pL(w):
            w.dma(b1t[:], ins[bk1])
            w.dma(b2t[:], ins[bk2])
        b.stage(sync=pL)
        h_d = b.dram("h_d", [8192, TT], BF16)
        hh = [sl(PA[0], 0), sl(PA[0], 1)]
        def evH(mi, ti, pp, w, mw, tw):
            w.activation(out=hh[mi % 2][:, ti*512:ti*512+tw], in_=pp[0:mw, 0:tw],
                         func=AF.Gelu, bias=b1t[:, mi:mi+1])
        def postH(w, mi):
            w.dma(h_d[mi*128:(mi+1)*128, :], hh[mi % 2][:, 0:TT])
        KO.mm_stream(b, ins[wk1], E, 8192, xn_fm, TT, ctx["psf"], ctx["wb"], evH,
                     evac_eng="scalar", post=postH)
        f_fm = [sl(PA[i // 2], i % 2) for i in range(16)]
        def evF(mi, ti, pp, w, mw, tw):
            w.tensor_scalar(out=f_fm[mi][:, ti*512:ti*512+tw], in0=pp[0:mw, 0:tw],
                            scalar1=b2t[:, mi:mi+1], scalar2=None, op0=ALU.add)
        xslots = [sl(PB[i], h) for i in range(8) for h in range(2)]
        KO.mm_xk(b, lambda mi, band: ins[wk2][(mi*64 + band*8)*128:(mi*64 + band*8 + 8)*128, :],
                 8192, E, h_d, TT, ctx["psf"], xslots, ctx["wb"], evF, tag=f"xk{wk1}")
        ftiles = PA[8:16]
        KO.transpose_fm_tm(b, f_fm, ftiles, 16, 8, ctx["psb"], ctx["ident"])
        ctx["ftiles"] = ftiles
    return mid

def build_l1b():
    extra = [("w1qT", [E, 8192], BF16), ("w2qT", [16 * 8192, 128], BF16),
             ("b1c", [128, 64], F32), ("b2c", [128, 16], F32),
             ("normw", [128, 32], F32), ("sqa", [1, TT], F32), ("sqb", [1, TT], F32)]
    def pre(b, ctx):
        ins = ctx["ins"]
        nw = b.sb("nw", [128, 32], F32)
        sq = ctx["tmpB"]
        Rb = ctx["tmpA"][:, 0:TT]
        onesf = b.sb("onesf1b", [1, 128], F32)
        def p0(w):
            w.dma(nw[:], ins["normw"])
            w.dma(sq[0:1, 0:TT], ins["sqa"])
            w.dma(sq[0:1, TT:2*TT], ins["sqb"])
        b.stage(sync=p0)
        def p1(w):
            w.memset(onesf[0:1, :], 1.0)
            w.tensor_add(sq[0:1, 0:TT], sq[0:1, 0:TT], sq[0:1, TT:2*TT])
            w.ev("l1b_sq")
        b.stage(vector=p1)
        def p2(w):
            w.activation(out=sq[0:1, TT:2*TT], in_=sq[0:1, 0:TT], func=AF.Sqrt,
                         scale=1.0/4096.0, bias=1e-6)
        b.stage(scalar=p2)
        def p3(w):
            w.reciprocal(sq[0:1, 0:TT], sq[0:1, TT:2*TT])
        b.stage(vector=p3)
        def p4(w):
            for j in range(TT // 512):
                w.matmul(ctx["psf"][j][:, :], onesf[0:1, 0:128],
                         sq[0:1, j*512:(j+1)*512], start=True, stop=True)
        b.stage(tensor=p4)
        def p5(w):
            for j in range(TT // 512):
                w.tensor_copy(Rb[:, j*512:(j+1)*512], ctx["psf"][j][:, :])
        b.stage(vector=p5)
        actt = ctx["actt"]
        for g in range(0, 32, 8):
            def p6(w, g=g):
                for i in range(g, g + 8):
                    w.tensor_scalar(out=actt[i], in0=actt[i],
                                    scalar1=nw[:, i:i+1], scalar2=None, op0=ALU.mult)
                    w.tensor_mul(actt[i], actt[i], Rb[:, 0:TT])
            b.stage(vector=p6)
    return build_B(4096, make_ffn_mid_l12("w1qT", "w2qT", "b1c", "b2c"), extra,
                   pre_proj=pre)

def prep_l1b(core, inp, T=None, wq=None, x2=None, o_h=None, sq_h=None):
    bb, j = core // 2, core % 2
    tsl = slice(j * 1024, (j + 1) * 1024)
    w1q = T["l1_w1q"] if wq is None else wq["l1w1"]
    w2q = T["l1_w2q"] if wq is None else wq["l1w2"]
    if o_h is None:
        act = np.asarray(T["l1_o"][bb]).T[:, tsl]
        sqa = T["l1_ssq"][bb, tsl].reshape(1, -1) * 0.5
        sqb = T["l1_ssq"][bb, tsl].reshape(1, -1) * 0.5
    else:
        act = np.concatenate([o_h[(bb, 0)], o_h[(bb, 1)]], 0)[:, tsl]
        sqa = sq_h[(bb, 0)][:, tsl]
        sqb = sq_h[(bb, 1)][:, tsl]
    xv = T["x2"] if x2 is None else x2
    return {
        "act": to_bf(act),
        "xin": to_f32(xv[bb, tsl]),
        "wproj": to_f32(inp["L1_attn_Wo"]),
        "identD": np.eye(128, dtype=ml_dtypes.bfloat16),
        "w1qT": to_bf(w1q.T), "w2qT": to_bf(np.stack(np.split(w1q_dummy := w2q.T, 16, axis=1)).reshape(-1, 128)),
        "b1c": to_f32(np.asarray(inp["L1_ffn_b1"]).reshape(64, 128).T),
        "b2c": to_f32(np.asarray(inp["L1_ffn_b2"]).reshape(16, 128).T),
        "normw": to_f32(np.asarray(inp["L1_attn_norm_w"]).reshape(32, 128).T),
        "sqa": to_f32(sqa),
        "sqb": to_f32(sqb),
    }

def build_l2b():
    extra = [("w1qT", [E, 8192], BF16), ("w2qT", [16 * 8192, 128], BF16),
             ("b1c", [128, 64], F32), ("b2c", [128, 16], F32)]
    return build_B(E, make_ffn_mid_l12("w1qT", "w2qT", "b1c", "b2c"), extra,
                   proj_bias=True, need_round=False)

def prep_l2b(core, inp, T=None, ao_full=None, x5_full=None, wq=None, x4=None):
    bb, j = core // 2, core % 2
    tsl = slice(j * 1024, (j + 1) * 1024)
    w1q = T["l2_w1q"] if wq is None else wq["l2w1"]
    w2q = T["l2_w2q"] if wq is None else wq["l2w2"]
    if ao_full is None:
        raise RuntimeError("need ao_full")
    xv = T["x4"] if x4 is None else x4
    return {
        "act": to_bf(ao_full[bb][:, tsl]),
        "xin": to_f32(xv[bb, tsl]),
        "wproj": to_f32(np.asarray(inp["L2_attn_out_w"]).T),
        "identD": np.eye(128, dtype=ml_dtypes.bfloat16),
        "pbias": to_bf(np.broadcast_to(inp["L2_attn_out_b"], (128, E))),
        "w1qT": to_bf(w1q.T),
        "w2qT": to_bf(np.stack(np.split(w2q.T, 16, axis=1)).reshape(-1, 128)),
        "b1c": to_f32(np.asarray(inp["L2_ffn_b1"]).reshape(64, 128).T),
        "b2c": to_f32(np.asarray(inp["L2_ffn_b2"]).reshape(16, 128).T),
    }


def ffn_mid_l3(b, ctx):
    x, PA, PB = ctx["x"], ctx["PA"], ctx["PB"]
    scr, tmpA, tmpB, qI = ctx["scr"], ctx["tmpA"], ctx["tmpB"], ctx["qI"]
    ins = ctx["ins"]
    Q = 127.0
    KO.ln_stage_stats(b, x, scr, ctx["junk"], E)
    KO.ln_stage_cols(b, scr, 8, E, 1e-5)
    xq = PA[8:16]
    for i in range(8):
        o = 8 * i
        def s1(w, i=i, o=o):
            w.reciprocal(scr[:, o+6:o+7], scr[:, o+5:o+6])
            w.tensor_scalar(out=tmpB[:], in0=x[i][:], scalar1=scr[:, o+2:o+3],
                            scalar2=scr[:, o+6:o+7], op0=ALU.subtract, op1=ALU.mult)
            w.reduce_max(scr[:, 64+2*i:64+2*i+1], tmpB[:], axis=AX.X,
                         apply_absolute_value=True)
            w.reciprocal(scr[:, 64+2*i+1:64+2*i+2], scr[:, 64+2*i:64+2*i+1])
            w.tensor_scalar(out=tmpB[:], in0=tmpB[:],
                            scalar1=scr[:, 64+2*i+1:64+2*i+2], scalar2=Q,
                            op0=ALU.mult, op1=ALU.mult)
            KO.round_emit(w, xq[i][:], tmpB[:], qI, tmpA, tmpB, E)
        b.stage(vector=s1)
    xq_fm = [sl(PB[i // 2], i % 2) for i in range(16)]
    KO.transpose_fm_tm(b, xq, xq_fm, 8, 16, ctx["psb"], ctx["ident"])
    amd = b.dram("amd", [8, 128], F32)
    amc = b.sb("amc", [128, 8], F32)
    arow = b.sb("arow", [1, 2 * TT], F32)
    am3t = b.sb("am3t", [1, 2], F32)
    b1t = b.sb("b1t3", [128, 64], F32)
    b2t = b.sb("b2t3", [128, 16], F32)
    onesr = b.sb("onesr3", [1, 128], F32)
    Amb = tmpA[:, 0:TT]
    def sA0(w):
        for i in range(8):
            w.tensor_copy(amc[:, i:i+1], scr[:, 64+2*i:64+2*i+1])
        w.memset(onesr[0:1, :], 1.0)
    b.stage(vector=sA0)
    identf = b.sb("identf", [128, 128], F32)
    junk2 = b.sb("junk2", [128, TT], F32)
    ctx["identD_f"] = identf
    ctx["junk2"] = junk2
    def sA0b(w):
        w.dma(identf[:], ins["identDf"])
    b.stage(sync=sA0b)
    def sA1(w):
        w.transpose(ctx["psf"][0][0:8, 0:128], amc[:, 0:8], identf[:, 0:128])
        w.ev("l3amT")
    amr8 = b.sb("amr8", [8, 128], F32)
    def sA2(w):
        w.wait("l3amT")
        w.tensor_copy(amr8[0:8, 0:128], ctx["psf"][0][0:8, 0:128])
    b.stage(tensor=sA1, vector=sA2)
    def sA3(w):
        w.dma(amd, amr8[0:8, 0:128])
        w.wait_dma()
        w.dma(arow[0:1, 0:TT], amd.rearrange("a b -> (a b)").rearrange("(a t) -> a t", a=1))
        w.dma(am3t[0:1, 0:2], ins["am3"])
    b.stage(gpsimd=sA3)
    def sA4(w):
        w.tensor_scalar(out=arow[0:1, TT:2*TT], in0=arow[0:1, 0:TT],
                        scalar1=am3t[0:1, 0:1], scalar2=1.0/(Q*Q),
                        op0=ALU.mult, op1=ALU.mult)
    b.stage(vector=sA4)
    def sA5(w):
        for j in range(TT // 512):
            w.matmul(ctx["psf"][j][:, :], onesr[0:1, 0:128],
                     arow[0:1, TT + j*512:TT + (j+1)*512], start=True, stop=True)
    b.stage(tensor=sA5)
    def sA6(w):
        for j in range(TT // 512):
            w.tensor_copy(Amb[:, j*512:(j+1)*512], ctx["psf"][j][:, :])
    b.stage(vector=sA6)
    def sA7(w):
        w.dma(b1t[:], ins["b1c"])
        w.dma(b2t[:], ins["b2c"])
    b.stage(sync=sA7)
    h_d = b.dram("h3_d", [8192, TT], BF16)
    hh = [sl(PA[0], 0), sl(PA[0], 1)]
    gpre = [sl(PA[1], 0), sl(PA[1], 1)]
    x2am = tmpB[:, TT:2*TT]
    wb = ctx["wb"]
    wdram = ins["w1qT"]
    nk, nm, ntc = 16, 64, 2
    psb = ctx["psf"]
    def load_chunk(w, mi, buf):
        for c0 in range(0, nk, 4):
            w.dma(out=buf[:, c0*128:(c0+4)*128].rearrange("p (c m) -> p c m", m=128),
                  in_=wdram[:, mi*128:(mi+1)*128].rearrange("(c p) m -> p c m", p=128)[:, c0:c0+4, :])
    b.stage(gpsimd=lambda w: load_chunk(w, 0, wb[0]))
    def init_am(w):
        w.memset(x2am[:, 0:TT], 0.0)
    b.stage(vector=init_am)
    for mi in range(nm + 2):
        def st(w, mi=mi):
            if mi < nm:
                buf = wb[mi % 2]
                for ti in range(ntc):
                    pp = psb[(mi % 2) * ntc + ti]
                    for kc in range(nk):
                        w.matmul(pp[:, 0:512], buf[:, kc*128:(kc+1)*128],
                                 xq_fm[kc][:, ti*512:(ti+1)*512],
                                 start=(kc == 0), stop=(kc == nk - 1))
        def sl_(w, mi=mi):
            if mi + 1 < nm:
                load_chunk(w, mi + 1, wb[(mi + 1) % 2])
        def sev(w, mi=mi):
            if 0 < mi <= nm:
                m2 = mi - 1
                for ti in range(ntc):
                    w.tensor_tensor(out=gpre[m2 % 2][:, ti*512:(ti+1)*512],
                                    in0=psb[(m2 % 2) * ntc + ti][:, 0:512],
                                    in1=Amb[:, ti*512:(ti+1)*512], op=ALU.mult)
                w.ev(f"l3g_{m2}")
            if 1 < mi <= nm + 1:
                am2 = mi - 2
                w.wait(f"l3h_{am2}")
                w.scalar_tensor_tensor(out=junk2[:, 0:TT], in0=hh[am2 % 2][:, 0:TT],
                                       scalar=-1.0, in1=hh[am2 % 2][:, 0:TT],
                                       op0=ALU.mult, op1=ALU.max)
                w.tensor_max(x2am[:, 0:TT], x2am[:, 0:TT], junk2[:, 0:TT])
                w.ev(f"l3a_{am2}")
        def sgl(w, mi=mi):
            if 0 < mi <= nm:
                m2 = mi - 1
                w.wait(f"l3g_{m2}")
                w.activation(out=hh[m2 % 2][:, 0:TT], in_=gpre[m2 % 2][:, 0:TT],
                             func=AF.Gelu_apprx_tanh, bias=b1t[:, m2:m2+1])
                w.ev(f"l3h_{m2}")
        def sp(w, mi=mi):
            if 1 < mi <= nm + 1:
                m2 = mi - 2
                w.wait(f"l3a_{m2}")
                w.dma(h_d[m2*128:(m2+1)*128, :], hh[m2 % 2][:, 0:TT])
        kw = {}
        if mi < nm:
            kw["tensor"] = st
        if 0 < mi <= nm + 1:
            kw["vector"] = sev
        if 0 < mi <= nm:
            kw["scalar"] = sgl
        gfns = []
        if mi + 1 < nm:
            gfns.append(sl_)
        if 1 < mi <= nm + 1:
            gfns.append(sp)
        if gfns:
            def gall(w, fns=tuple(gfns)):
                for f in fns:
                    f(w)
            kw["gpsimd"] = gall
        if kw:
            b.stage(**kw)
    vt = junk2[0:32, 0:TT]
    r32t = b.sb("r32t", [32, 32], F32)
    def ra1(w):
        w.tensor_copy(junk2[0:64, 0:TT], x2am[64:128, 0:TT])
        w.tensor_max(x2am[0:64, 0:TT], x2am[0:64, 0:TT], junk2[0:64, 0:TT])
        w.tensor_copy(junk2[0:32, 0:TT], x2am[32:64, 0:TT])
        w.tensor_max(x2am[0:32, 0:TT], x2am[0:32, 0:TT], junk2[0:32, 0:TT])
        w.transpose(vt[0:32, 0:TT], x2am[0:32, 0:TT])
    b.stage(vector=ra1)
    def ra2(w):
        w.tensor_reduce(r32t[0:32, 0:32],
                        vt[0:32, 0:TT].rearrange("p (b i) -> p b i", i=32),
                        axis=AX.X, op=ALU.max)
    b.stage(vector=ra2)
    am2d = b.dram("am2d", [32, 32], F32)
    r32p = b.sb("r32p", [32, 32], F32)
    def ra3(w):
        w.transpose(ctx["psf"][0][0:32, 0:32], r32t[0:32, 0:32], ctx["identD_f"][0:32, 0:32])
        w.ev("l3am2T")
    def ra3e(w):
        w.wait("l3am2T")
        w.tensor_copy(r32p[0:32, 0:32], ctx["psf"][0][0:32, 0:32])
    b.stage(tensor=ra3, vector=ra3e)
    def ra4(w):
        w.dma(am2d, r32p[0:32, 0:32])
        w.wait_dma()
        w.dma(arow[0:1, 0:TT], am2d.rearrange("a b -> (a b)").rearrange("(a t) -> a t", a=1))
    b.stage(gpsimd=ra4)
    Qb = tmpA[:, 0:TT]
    Am2b = tmpA[:, TT:2*TT]
    def ra5(w):
        w.reciprocal(arow[0:1, TT:2*TT], arow[0:1, 0:TT])
        w.tensor_scalar(out=arow[0:1, TT:2*TT], in0=arow[0:1, TT:2*TT],
                        scalar1=Q, scalar2=None, op0=ALU.mult)
        w.tensor_scalar(out=arow[0:1, 0:TT], in0=arow[0:1, 0:TT],
                        scalar1=am3t[0:1, 1:2], scalar2=1.0/(Q*Q),
                        op0=ALU.mult, op1=ALU.mult)
    b.stage(vector=ra5)
    def ra6(w):
        w.matmul(ctx["psf"][0][:, :], onesr[0:1, 0:128], arow[0:1, TT:TT+512],
                 start=True, stop=True)
        w.matmul(ctx["psf"][1][:, :], onesr[0:1, 0:128], arow[0:1, TT+512:2*TT],
                 start=True, stop=True)
        w.matmul(ctx["psf"][2][:, :], onesr[0:1, 0:128], arow[0:1, 0:512],
                 start=True, stop=True)
        w.matmul(ctx["psf"][3][:, :], onesr[0:1, 0:128], arow[0:1, 512:TT],
                 start=True, stop=True)
        w.ev("l3qb")
    def ra6e(w):
        w.wait("l3qb")
        w.tensor_copy(Qb[:, 0:512], ctx["psf"][0][:, :])
        w.tensor_copy(Qb[:, 512:TT], ctx["psf"][1][:, :])
        w.tensor_copy(Am2b[:, 0:512], ctx["psf"][2][:, :])
        w.tensor_copy(Am2b[:, 512:TT], ctx["psf"][3][:, :])
    b.stage(tensor=ra6, vector=ra6e)
    x2q_d = b.dram("x2q_d", [8192, TT], BF16)
    hbq = [[sl(PB[0], 0), sl(PB[0], 1)]]
    obq = [[sl(PB[1], 0), sl(PB[1], 1)]]
    def qfn(w, k, tins, touts):
        w.tensor_tensor(out=tmpB[:, 0:TT], in0=tins[0][:, 0:TT], in1=Qb[:, 0:TT],
                        op=ALU.mult)
        KO.round_emit(w, touts[0][:, 0:TT], tmpB[:, 0:TT], qI,
                      ctx["junk2"], tmpB, TT)
    KO.ew_stream(b, [h_d], [x2q_d], 64, TT, hbq, obq, qfn)
    f_fm = [sl(PA[i // 2], i % 2) for i in range(16)]
    xslots = [sl(PB[i], h) for i in range(8) for h in range(2)]
    def evF(mi, ti, pp, w, mw, tw):
        w.tensor_tensor(out=f_fm[mi][:, ti*512:ti*512+tw], in0=pp[0:mw, 0:tw],
                        in1=Am2b[:, ti*512:ti*512+tw], op=ALU.mult)
        w.tensor_scalar(out=f_fm[mi][:, ti*512:ti*512+tw],
                        in0=f_fm[mi][:, ti*512:ti*512+tw],
                        scalar1=b2t[:, mi:mi+1], scalar2=None, op0=ALU.add)
    KO.mm_xk(b, lambda mi, band: ins["w2qTc"][(mi*64 + band*8)*128:(mi*64 + band*8 + 8)*128, :],
             8192, E, x2q_d, TT, ctx["psf"], xslots, ctx["wb"], evF, tag="xkl3")
    ftiles = PA[8:16]
    KO.transpose_fm_tm(b, f_fm, ftiles, 16, 8, ctx["psb"], ctx["ident"])
    ctx["ftiles"] = ftiles

def build_l3b():
    extra = [("w1qT", [E, 8192], BF16), ("w2qTc", [16 * 8192, 128], BF16),
             ("b1c", [128, 64], F32), ("b2c", [128, 16], F32), ("am3", [1, 2], F32),
             ("identDf", [128, 128], F32)]
    return build_B(E, ffn_mid_l3, extra, need_round=True)

def prep_l3b(core, inp, T=None, ao_full=None, wq=None, x6=None):
    bb, j = core // 2, core % 2
    tsl = slice(j * 1024, (j + 1) * 1024)
    w1q = T["l3_w1q"] if wq is None else wq["l3w1"]
    w2q = T["l3_w2q"] if wq is None else wq["l3w2"]
    am = np.array([[T["l3_w1q_am"], T["l3_w2q_am"]]], np.float32) if wq is None \
        else wq["am3"]
    if ao_full is None:
        raise RuntimeError("need ao_full")
    xv = T["x6"] if x6 is None else x6
    return {
        "act": to_bf(ao_full[bb][:, tsl]),
        "xin": to_f32(xv[bb, tsl]),
        "wproj": to_f32(inp["L3_attn_w_proj"]),
        "identD": np.eye(128, dtype=ml_dtypes.bfloat16),
        "w1qT": to_bf(w1q.T),
        "w2qTc": to_bf(np.stack(np.split(w2q.T, 16, axis=1)).reshape(-1, 128)),
        "b1c": to_f32(np.asarray(inp["L3_ffn_B1"]).reshape(64, 128).T),
        "b2c": to_f32(np.asarray(inp["L3_ffn_B2"]).reshape(16, 128).T),
        "am3": to_f32(am),
        "identDf": np.eye(128, dtype=np.float32),
    }


def emit_attention(b, ctx, nheads, qfm_tiles, kfm_tiles, v_tm, dv_off, psf, psb,
                   ident, maskd, ao_h, store_h, qk2=None, tag="att"):
    nqb = 16
    scrA = b.sb(f"scrA_{tag}", [128, 24], F32)
    P = ctx["P"]
    PT = ctx["PT"]
    P2b = ctx.get("P2") or b.sb(f"P2_{tag}", [128, 2048], BF16)
    psv = b.ps(f"psv_{tag}", [128, 128], F32)
    PB = lambda qb: (P if qb % 2 == 0 else P2b)
    for h in range(nheads):
        for st in range(nqb + 1):
            cur = st if st < nqb else None
            prev = st - 1 if st >= 1 else None
            def tn(w, h=h, cur=cur, prev=prev):
                if cur is not None:
                    nkt = cur + 1
                    nch = (nkt * 128 + 511) // 512
                    for c in range(nch):
                        k0 = c * 512
                        kw_ = min(512, nkt * 128 - k0)
                        first = True
                        for (qap, kap) in qk2[h]:
                            w.matmul(psf[c][:, 0:kw_], qap(cur), kap(k0, kw_),
                                     start=first, stop=(qap, kap) == qk2[h][-1])
                            first = False
                    w.ev(f"{tag}S_{h}_{cur}")
                if prev is not None:
                    nktp = prev + 1
                    ngp = (nktp + 7) // 8
                    w.wait(f"{tag}N_{h}_{prev}")
                    for g in range(ngp):
                        if g >= 2:
                            w.wait(f"{tag}e_{h}_{prev}_{g-2}")
                        for kt in range(g*8, min(nktp, (g+1)*8)):
                            w.transpose(psb[g % 2][:, (kt - g*8)*128:(kt - g*8 + 1)*128],
                                        PB(prev)[:, kt*128:(kt+1)*128], ident[:, 0:128])
                        w.ev(f"{tag}t_{h}_{prev}_{g}")
                    for kt in range(nktp):
                        w.wait(f"{tag}e_{h}_{prev}_{kt // 8}")
                        w.matmul(psv[:, 0:128], v_tm[kt][:, dv_off(h):dv_off(h)+128],
                                 PT[:, kt*128:(kt+1)*128], start=(kt == 0),
                                 stop=(kt == nktp - 1))
                    w.ev(f"{tag}o_{h}_{prev}")
            def vc(w, h=h, cur=cur, prev=prev):
                if prev is not None:
                    nktp = prev + 1
                    nchp = (nktp * 128 + 511) // 512
                    op = 12 * (prev % 2)
                    w.reduce_sum(scrA[:, op+6:op+7], scrA[:, op+8:op+8+nchp],
                                 axis=AX.X)
                    w.reciprocal(scrA[:, op+7:op+8], scrA[:, op+6:op+7])
                    w.tensor_scalar(out=PB(prev)[:, 0:nktp*128],
                                    in0=PB(prev)[:, 0:nktp*128],
                                    scalar1=scrA[:, op+7:op+8], scalar2=None,
                                    op0=ALU.mult)
                    w.ev(f"{tag}N_{h}_{prev}")
                if cur is not None:
                    nkt = cur + 1
                    nch = (nkt * 128 + 511) // 512
                    oc = 12 * (cur % 2)
                    w.wait(f"{tag}S_{h}_{cur}")
                    dc = (nkt - 1) * 128 - (nch - 1) * 512
                    w.tensor_tensor(out=psf[nch-1][:, dc:dc+128],
                                    in0=psf[nch-1][:, dc:dc+128],
                                    in1=maskd[:, 0:128], op=ALU.add)
                    for c in range(nch):
                        kw_ = min(512, nkt * 128 - c * 512)
                        w.reduce_max(scrA[:, oc+c:oc+c+1], psf[c][:, 0:kw_],
                                     axis=AX.X)
                    w.tensor_reduce(scrA[:, oc+4:oc+5], scrA[:, oc:oc+nch],
                                    axis=AX.X, op=ALU.max)
                    w.tensor_scalar(out=scrA[:, oc+5:oc+6], in0=scrA[:, oc+4:oc+5],
                                    scalar1=-1.0, scalar2=None, op0=ALU.mult)
                    w.ev(f"{tag}M_{h}_{cur}")
                if prev is not None:
                    nktp = prev + 1
                    ngp = (nktp + 7) // 8
                    for g in range(ngp):
                        w.wait(f"{tag}t_{h}_{prev}_{g}")
                        nk_ = min(nktp, (g+1)*8) - g*8
                        w.tensor_copy(PT[:, g*8*128:(g*8+nk_)*128],
                                      psb[g % 2][:, 0:nk_*128])
                        w.ev(f"{tag}e_{h}_{prev}_{g}")
                    w.wait(f"{tag}o_{h}_{prev}")
                    w.tensor_copy(ao_h[:, prev*128:(prev+1)*128], psv[:, 0:128])
            def sc(w, h=h, cur=cur):
                if cur is not None:
                    nkt = cur + 1
                    nch = (nkt * 128 + 511) // 512
                    oc = 12 * (cur % 2)
                    w.wait(f"{tag}M_{h}_{cur}")
                    for c in range(nch):
                        kw_ = min(512, nkt * 128 - c * 512)
                        w.activation(out=PB(cur)[:, c*512:c*512+kw_],
                                     in_=psf[c][:, 0:kw_], func=AF.Exp,
                                     bias=scrA[:, oc+5:oc+6],
                                     accum_out=scrA[:, oc+8+c:oc+9+c])
                    w.ev(f"{tag}X_{h}_{cur}")
            kw = {"tensor": tn, "vector": vc}
            if cur is not None:
                kw["scalar"] = sc
            b.stage(**kw)
        def sS(w, h=h):
            store_h(w, h)
        b.stage(gpsimd=sS)


def build_l2a():
    b = Builder()
    ins = {}
    ins["xT"] = b.dram("xT", [E, E], F32, kind="ExternalInput")
    ins["wqkT"] = b.dram("wqkT", [E, E], F32, kind="ExternalInput")
    ins["wvT"] = b.dram("wvT", [E, 1024], F32, kind="ExternalInput")
    ins["qkbias"] = b.dram("qkbias", [128, 16], F32, kind="ExternalInput")
    ins["vbias"] = b.dram("vbias", [128, 1024], BF16, kind="ExternalInput")
    ins["embT"] = b.dram("embT", [128, E], BF16, kind="ExternalInput")
    ins["identD"] = b.dram("identD", [128, 128], BF16, kind="ExternalInput")
    ins["maskd"] = b.dram("maskd", [128, 128], F32, kind="ExternalInput")
    ao_out = b.dram("ao_out", [1024, E], BF16, kind="ExternalOutput")

    xn = [b.sb(f"xn{i}", [128, E], BF16) for i in range(16)]
    qk = [b.sb(f"qk{i}", [128, E], BF16) for i in range(16)]
    v_tm = [b.sb(f"vt{i}", [128, 1024], BF16) for i in range(16)]
    sq2 = [b.sb(f"sq2{i}", [128, E], BF16) for i in range(2)]
    srow = b.sb("srow", [1, 2 * E], F32)
    onesf = b.sb("onesf", [1, 128], F32)
    onesbf = b.sb("onesbf", [128, 1], BF16)
    wb = [b.sb(f"wb{i}", [128, 16 * 128], BF16) for i in range(2)]
    qkb = b.sb("qkb", [128, 16], F32)
    embT = b.sb("embT_t", [128, E], BF16)
    ident = b.sb("ident", [128, 128], BF16)
    maskd = b.sb("maskd_t", [128, 128], F32)
    ao_h = b.sb("ao_h", [128, E], BF16)
    psf = [b.ps(f"psf{i}", [128, 512], F32) for i in range(4)]
    psb = [b.ps(f"psb{i}", [128, 1024], BF16) for i in range(2)]

    def pA(w):
        for i in range(16):
            w.dma(xn[i][:], ins["xT"][i*128:(i+1)*128, :])
        w.dma(qkb[:], ins["qkbias"])
        w.dma(embT[:], ins["embT"])
        w.dma(ident[:], ins["identD"])
        w.dma(maskd[:], ins["maskd"])
    b.stage(gpsimd=pA)
    def pA2(w):
        w.memset(onesf[0:1, :], 1.0)
        w.memset(onesbf[:, 0:1], 1.0)
    b.stage(vector=pA2)
    KO.ln_fm(b, xn, E, psf, srow, onesf, sq2, onesb=onesbf)
    embTq = b.sb("embTq", [128, E], BF16)
    vbias = b.sb("vbias_t", [128, 1024], BF16)
    def pA3(w):
        w.tensor_scalar(out=embTq[:, 0:E], in0=embT[:, 0:E],
                        scalar1=float(128 ** -0.5), scalar2=None, op0=ALU.mult)
    b.stage(vector=pA3)
    def pA4(w):
        w.dma(vbias[:, 0:1024], ins["vbias"])
    b.stage(sync=pA4)
    def evQK(mi, ti, pp, w, mw, tw):
        et = embTq if mi < 8 else embT
        w.scalar_tensor_tensor(out=qk[mi][:, ti*512:ti*512+tw], in0=pp[0:mw, 0:tw],
                               scalar=qkb[:, mi:mi+1], in1=et[:, ti*512:ti*512+tw],
                               op0=ALU.add, op1=ALU.mult)
    KO.mm_stream(b, ins["wqkT"], E, E, xn, E, psf, wb, evQK)
    def evV(ci, tt, pp, w, mw):
        w.tensor_tensor(out=v_tm[tt][:, ci*128:ci*128+mw], in0=pp[:, 0:mw],
                        in1=vbias[:, ci*128:ci*128+mw], op=ALU.add)
    KO.mm_tm(b, ins["wvT"], E, 1024, xn, E, psf, wb, evV, tag="l2v")
    def store_h(w, h):
        w.dma(ao_out[h*128:(h+1)*128, :], ao_h[:, 0:E])
        w.wait_dma()
    qk2 = {h: [(lambda qb, h=h: qk[h][:, qb*128:(qb+1)*128],
                lambda k0, kw_, h=h: qk[8 + h][:, k0:k0+kw_])] for h in range(8)}
    actx = {"P": sq2[0], "PT": sq2[1], "P2": xn[0]}
    emit_attention(b, actx, 8, None, None, v_tm, lambda h: h * 128, psf, psb,
                   ident, maskd, ao_h, store_h, qk2=qk2, tag="l2")
    b.stage(gpsimd=lambda w: w.wait_dma())
    return b

def prep_l2a(core, inp, T=None, x4=None):
    bb, j = core // 2, core % 2
    hsl = slice(j * 1024, (j + 1) * 1024)
    qkvw = np.asarray(inp["L2_attn_qkv_w"], np.float32)
    qkvb = np.asarray(inp["L2_attn_qkv_b"], np.float32)
    wq = qkvw[hsl].T
    wk = qkvw[2048:4096][hsl].T
    wv = qkvw[4096:6144][hsl].T
    bq = qkvb[hsl]; bk = qkvb[2048:4096][hsl]; bv = qkvb[4096:6144][hsl]
    t = 2048; D = 128
    fr = np.arange(0, D, 2, dtype=np.float32) / D
    ang = np.arange(t, dtype=np.float32)[:, None] * fr[None, :]
    emb = np.concatenate([np.sin(ang), np.cos(ang)], -1)
    maskd = np.triu(np.full((128, 128), -1e9, np.float32), 1)
    qkbias = np.concatenate([bq, bk]).reshape(16, 128).T
    xv = T["x4"] if x4 is None else x4
    return {
        "xT": to_f32(np.asarray(xv[bb]).T),
        "wqkT": to_f32(np.concatenate([wq, wk], 1)),
        "wvT": to_f32(wv),
        "qkbias": to_f32(qkbias),
        "vbias": to_bf(np.broadcast_to(bv, (128, 1024))),
        "embT": to_bf(emb.T),
        "identD": np.eye(128, dtype=ml_dtypes.bfloat16),
        "maskd": maskd,
    }


def build_l3a():
    b = Builder()
    ins = {}
    for nm, shp, dt in [("xT", [E, E], F32), ("wcq", [E, 1536], F32),
                        ("qnwc", [128, 12], F32), ("wdqn", [1536, 1024], F32),
                        ("wdqr", [1536, 512], F32), ("wckv", [E, 512], F32),
                        ("kvnwc", [128, 4], F32), ("wdkn", [512, 1024], F32),
                        ("wdv", [512, 1024], F32), ("wkrope", [E, 64], F32),
                        ("ctab", [128, 4], F32), ("stab", [128, 4], F32),
                        ("JT2", [128, 128], BF16), ("JT", [64, 64], BF16),
                        ("identD", [128, 128], BF16), ("maskd", [128, 128], F32)]:
        ins[nm] = b.dram(nm, shp, dt, kind="ExternalInput")
    ao_out = b.dram("ao_out", [1024, E], BF16, kind="ExternalOutput")

    xn = [b.sb(f"xn{i}", [128, E], BF16) for i in range(16)]
    cq = [b.sb(f"cq{i}", [128, E], BF16) for i in range(12)]
    nkv = [b.sb(f"nkv{i}", [128, E], BF16) for i in range(4)]
    qr = [b.sb(f"qr{i}", [128, E], BF16) for i in range(4)]
    krot = [b.sb(f"krot{i}", [128, E], BF16) for i in range(4)]
    kpair = b.sb("kpair", [128, E], BF16)
    jkp = b.sb("jkp", [128, E], BF16)
    krp = b.sb("krp", [64, E], BF16)
    sq2 = [b.sb(f"sq2{i}", [128, E], BF16) for i in range(2)]
    srow = b.sb("srow", [1, 2 * E], F32)
    onesf = b.sb("onesf", [1, 128], F32)
    onesbf = b.sb("onesbf", [128, 1], BF16)
    wb = [b.sb(f"wb{i}", [128, 16 * 128], BF16) for i in range(2)]
    qnw = b.sb("qnw", [128, 12], F32)
    kvnw = b.sb("kvnw", [128, 4], F32)
    ctab = b.sb("ctab_t", [128, 4], F32)
    stab = b.sb("stab_t", [128, 4], F32)
    JT2 = b.sb("JT2_t", [128, 128], BF16)
    JTs = b.sb("JT_t", [64, 64], BF16)
    ident = b.sb("ident", [128, 128], BF16)
    maskd = b.sb("maskd_t", [128, 128], F32)
    psf = [b.ps(f"psf{i}", [128, 512], F32) for i in range(4)]
    psb = [b.ps(f"psb{i}", [128, 1024], BF16) for i in range(2)]
    ao_h = jkp
    kn = xn[0:8]
    qn = xn[8:16]
    v_tm = [sl(cq[i // 2], i % 2) for i in range(16)]

    def pA(w):
        for i in range(16):
            w.dma(xn[i][:], ins["xT"][i*128:(i+1)*128, :])
        for t, nm in [(qnw, "qnwc"), (kvnw, "kvnwc"), (ctab, "ctab"),
                      (stab, "stab"), (JT2, "JT2"), (JTs, "JT"),
                      (ident, "identD"), (maskd, "maskd")]:
            w.dma(t[:], ins[nm])
    b.stage(gpsimd=pA)
    def pA2(w):
        w.memset(onesf[0:1, :], 1.0)
        w.memset(onesbf[:, 0:1], 1.0)
    b.stage(vector=pA2)
    KO.ln_fm(b, xn, E, psf, srow, onesf, sq2, onesb=onesbf)
    def evCQ(mi, ti, pp, w, mw, tw):
        w.tensor_copy(cq[mi][:, ti*512:ti*512+tw], pp[0:mw, 0:tw])
    KO.mm_stream(b, ins["wcq"], E, 1536, xn, E, psf, wb, evCQ)
    def evKV(mi, ti, pp, w, mw, tw):
        w.tensor_copy(nkv[mi][:, ti*512:ti*512+tw], pp[0:mw, 0:tw])
    KO.mm_stream(b, ins["wckv"], E, 512, xn, E, psf, wb, evKV)
    def evKR(mi, ti, pp, w, mw, tw):
        w.tensor_copy(krp[0:64, ti*512:ti*512+tw], pp[0:mw, 0:tw])
    KO.mm_stream(b, ins["wkrope"], E, 64, xn, E, psf, wb, evKR)
    KO.rms_fm(b, cq, 1536, E, psf, srow, onesf, onesbf, sq2, qnw)
    KO.rms_fm(b, nkv, 512, E, psf, srow, onesf, onesbf, sq2, kvnw)
    SC = float(192 ** -0.5)
    def evQN(mi, ti, pp, w, mw, tw):
        w.tensor_scalar(out=qn[mi][:, ti*512:ti*512+tw], in0=pp[0:mw, 0:tw],
                        scalar1=SC, scalar2=None, op0=ALU.mult)
    KO.mm_stream(b, ins["wdqn"], 1536, 1024, cq, E, psf, wb, evQN)
    def evQR(mi, ti, pp, w, mw, tw):
        w.tensor_scalar(out=qr[mi][:, ti*512:ti*512+tw], in0=pp[0:mw, 0:tw],
                        scalar1=SC, scalar2=None, op0=ALU.mult)
    KO.mm_stream(b, ins["wdqr"], 1536, 512, cq, E, psf, wb, evQR)
    def evKN(mi, ti, pp, w, mw, tw):
        w.tensor_copy(kn[mi][:, ti*512:ti*512+tw], pp[0:mw, 0:tw])
    KO.mm_stream(b, ins["wdkn"], 512, 1024, nkv, E, psf, wb, evKN)
    VS = float(2048 ** -0.5)
    def evV(ci, tt, pp, w, mw):
        w.tensor_scalar(out=v_tm[tt][:, ci*128:ci*128+mw], in0=pp[:, 0:mw],
                        scalar1=VS, scalar2=None, op0=ALU.mult)
    KO.mm_tm(b, ins["wdv"], 512, 1024, nkv, E, psf, wb, evV, tag="l3v")
    def pK1(w):
        w.dma(kpair[0:64, :], krp[0:64, :])
        w.dma(kpair[64:128, :], krp[0:64, :])
        w.wait_dma()
    b.stage(gpsimd=pK1)
    for sl4 in range(4):
        def pK2(w, sl4=sl4):
            w.matmul(psf[sl4][:, :], JT2[:, 0:128], kpair[:, sl4*512:(sl4+1)*512],
                     start=True, stop=True)
            w.ev(f"l3jk_{sl4}")
        def pK2e(w, sl4=sl4):
            w.wait(f"l3jk_{sl4}")
            w.tensor_copy(jkp[:, sl4*512:(sl4+1)*512], psf[sl4][:, :])
        b.stage(tensor=pK2, vector=pK2e)
    def pK3(w):
        for p in range(4):
            w.tensor_scalar(out=krot[p][:, 0:E], in0=kpair[:, 0:E],
                            scalar1=ctab[:, p:p+1], scalar2=None, op0=ALU.mult)
            w.scalar_tensor_tensor(out=krot[p][:, 0:E], in0=jkp[:, 0:E],
                                   scalar=stab[:, p:p+1], in1=krot[p][:, 0:E],
                                   op0=ALU.mult, op1=ALU.add)
    b.stage(vector=pK3)
    for p in range(4):
        for sl4 in range(4):
            def pQ1(w, p=p, sl4=sl4):
                w.matmul(psf[sl4][:, :], JT2[:, 0:128], qr[p][:, sl4*512:(sl4+1)*512],
                         start=True, stop=True)
                w.ev(f"l3jq_{p}_{sl4}")
        def pQall(w, p=p):
            for sl4 in range(4):
                w.matmul(psf[sl4][:, :], JT2[:, 0:128], qr[p][:, sl4*512:(sl4+1)*512],
                         start=True, stop=True)
            w.ev(f"l3jq_{p}")
        def pQe(w, p=p):
            w.wait(f"l3jq_{p}")
            w.tensor_scalar(out=qr[p][:, 0:E], in0=qr[p][:, 0:E],
                            scalar1=ctab[:, p:p+1], scalar2=None, op0=ALU.mult)
            for sl4 in range(4):
                w.scalar_tensor_tensor(out=qr[p][:, sl4*512:(sl4+1)*512],
                                       in0=psf[sl4][:, :], scalar=stab[:, p:p+1],
                                       in1=qr[p][:, sl4*512:(sl4+1)*512],
                                       op0=ALU.mult, op1=ALU.add)
        b.stage(tensor=pQall, vector=pQe)
    def store_h(w, h):
        w.dma(ao_out[h*128:(h+1)*128, :], ao_h[:, 0:E])
        w.wait_dma()
    qk2 = {}
    for h in range(8):
        p, sub = h // 2, h % 2
        qk2[h] = [
            (lambda qb, h=h: qn[h][:, qb*128:(qb+1)*128],
             lambda k0, kw_, h=h: kn[h][:, k0:k0+kw_]),
            (lambda qb, p=p, sub=sub: qr[p][sub*64:sub*64+64, qb*128:(qb+1)*128],
             lambda k0, kw_, p=p, sub=sub: krot[p][sub*64:sub*64+64, k0:k0+kw_]),
        ]
    actx = {"P": sq2[0], "PT": sq2[1], "P2": wb[0][:, 0:2048]}
    emit_attention(b, actx, 8, None, None, v_tm, lambda h: h * 128, psf, psb,
                   ident, maskd, ao_h, store_h, qk2=qk2, tag="l3")
    b.stage(gpsimd=lambda w: w.wait_dma())
    return b

def prep_l3a(core, inp, T=None, x6=None):
    bb, j = core // 2, core % 2
    g = lambda n: np.asarray(inp[n], np.float32)
    hs = slice(j * 8, (j + 1) * 8)
    nsl = slice(j * 1024, (j + 1) * 1024)
    rsl = slice(j * 512, (j + 1) * 512)
    fc = g("L3_attn_freqs_cos")[hs]
    fs = g("L3_attn_freqs_sin")[hs]
    ct = np.repeat(fc, 2, axis=1)
    st_ = np.repeat(fs, 2, axis=1)
    ctab = np.zeros((128, 4), np.float32)
    stab = np.zeros((128, 4), np.float32)
    for p in range(4):
        ctab[0:64, p] = ct[2*p]; ctab[64:128, p] = ct[2*p+1]
        stab[0:64, p] = st_[2*p]; stab[64:128, p] = st_[2*p+1]
    J = np.zeros((64, 64), np.float32)
    for q in range(32):
        J[2*q, 2*q+1] = -1.0
        J[2*q+1, 2*q] = 1.0
    JT = J.T
    JT2 = np.zeros((128, 128), np.float32)
    JT2[0:64, 0:64] = JT; JT2[64:128, 64:128] = JT
    maskd = np.triu(np.full((128, 128), -1e9, np.float32), 1)
    xv = T["x6"] if x6 is None else x6
    return {
        "xT": to_f32(np.asarray(xv[bb]).T),
        "wcq": to_f32(inp["L3_attn_w_cq"]),
        "qnwc": to_f32(g("L3_attn_q_norm_w").reshape(12, 128).T),
        "wdqn": to_f32(g("L3_attn_w_dq_nope")[:, nsl]),
        "wdqr": to_f32(g("L3_attn_w_dq_rope")[:, rsl]),
        "wckv": to_f32(inp["L3_attn_w_ckv"]),
        "kvnwc": to_f32(g("L3_attn_kv_norm_w").reshape(4, 128).T),
        "wdkn": to_f32(g("L3_attn_w_dk_nope")[:, nsl]),
        "wdv": to_f32(g("L3_attn_w_dv")[:, nsl]),
        "wkrope": to_f32(inp["L3_attn_w_k_rope"]),
        "ctab": ctab, "stab": stab,
        "JT2": JT2.astype(ml_dtypes.bfloat16), "JT": JT.astype(ml_dtypes.bfloat16),
        "identD": np.eye(128, dtype=ml_dtypes.bfloat16),
        "maskd": maskd,
    }


def build_l0a_p1a():
    b = Builder()
    ins = {}
    for nm, shp in [("xT", [E, E]), ("wqkvT", [E, 4096]), ("wzT", [E, E]),
                    ("wbT", [E, 16]), ("waT", [E, 16])]:
        ins[nm] = b.dram(nm, shp, F32, kind="ExternalInput")
    pre_d = b.dram("pre_d", [4096, E], F32, kind="ExternalOutput")
    z_d = b.dram("z_d", [E, E], BF16, kind="ExternalOutput")
    ba_d = b.dram("ba_d", [32, E], F32, kind="ExternalOutput")
    xn = [b.sb(f"xn{i}", [128, E], F32) for i in range(16)]
    sq2 = [b.sb(f"sq2{i}", [128, E], F32) for i in range(2)]
    srow = b.sb("srow", [1, 2 * E], F32)
    onesf = b.sb("onesf", [1, 128], F32)
    onesbc = b.sb("onesbc", [128, 1], F32)
    wb = [b.sb(f"wb{i}", [128, 16 * 128], F32) for i in range(2)]
    psf = [b.ps(f"psf{i}", [128, 512], F32) for i in range(6)]
    def pA(w):
        for i in range(16):
            w.dma(xn[i][:], ins["xT"][i*128:(i+1)*128, :])
    b.stage(gpsimd=pA)
    def pA2(w):
        w.memset(onesf[0:1, :], 1.0)
        w.memset(onesbc[:, 0:1], 1.0)
    b.stage(vector=pA2)
    KO.ln_fm(b, xn, E, psf, srow, onesf, sq2, onesb=onesbc)
    C = dict(xn=xn, psf=psf, wb=wb, pre_d=pre_d, z_d=z_d)
    build_l0a_body(b, ins, C)
    bb_t, aa_t = C["bb_t"], C["aa_t"]
    def pBA(w):
        w.dma(ba_d[0:16, :], bb_t[0:16, 0:E])
        w.dma(ba_d[16:32, :], aa_t[0:16, 0:E])
        w.wait_dma()
    b.stage(gpsimd=pBA)
    return b

def build_l0a_p1b():
    b = Builder()
    ins = {}
    for nm, shp in [("convw", [128, 128]), ("convb", [128, 32]),
                    ("alog", [16, 1]), ("dtb", [16, 1]),
                    ("ws_l1w1", [1024, E]), ("ws_l1w2", [256, 8192]),
                    ("ws_l2w1", [1024, E]), ("ws_l2w2", [256, 8192]),
                    ("ws_l3w1", [1024, E]), ("ws_l3w2", [256, 8192]),
                    ("pre_d", [4096, E]), ("ba_d", [32, E])]:
        ins[nm] = b.dram(nm, shp, F32, kind="ExternalInput")
    pre_d, ba_d = ins["pre_d"], ins["ba_d"]
    conv_d = b.dram("conv_d", [4096, E], F32, kind="ExternalOutput")
    tab_d = {nm: b.dram(f"tab_{nm}", [16, E], F32, kind="ExternalOutput")
             for nm in L0A_TABS}
    ssc_d = b.dram("ssc_d", [16, 32], F32, kind="ExternalOutput")
    stat_out = b.dram("stat_out", [1, 8], F32, kind="ExternalOutput")
    g_d = b.dram("g_d", [16, E], F32)
    gc_d = b.dram("gc_d", [16, E], F32)
    lk_d = b.dram("lk_d", [8, E], F32)
    lq_d = b.dram("lq_d", [8, E], F32)
    srow = b.sb("srow", [1, 2 * E], F32)
    onesf = b.sb("onesf", [1, 128], F32)
    onesbc = b.sb("onesbc", [128, 1], F32)
    sq2 = [b.sb(f"sq2{i}", [128, E], F32) for i in range(2)]
    wld = [b.sb("wld0", [128, 8192], F32), b.sb("wld1", [128, 8192], F32)]
    b._l0ctx_wld = wld
    bb_t = b.sb("bb_t", [16, E], F32)
    cacc = b.sb("cacc", [128, E], F32)
    lk16t = b.sb("lk16t", [16, E], F32)
    lq16t = b.sb("lq16t", [16, E], F32)
    psf = [b.ps(f"psf{i}", [128, 512], F32) for i in range(6)]
    def pA2(w):
        w.memset(onesf[0:1, :], 1.0)
        w.memset(onesbc[:, 0:1], 1.0)
    b.stage(vector=pA2)
    wstat = b.sb("wstat", [128, 64], F32)
    jobs = [("ws_l1w1", 1024, E, 0), ("ws_l1w2", 256, 8192, 1),
            ("ws_l2w1", 1024, E, 2), ("ws_l2w2", 256, 8192, 3),
            ("ws_l3w1", 1024, E, 4), ("ws_l3w2", 256, 8192, 5)]
    tjobs = []
    for nm, rows, cols, idx in jobs:
        for i in range(rows // 128):
            tjobs.append((nm, i, cols, idx))
    def wstm(w):
        w.memset(wstat[:, 0:64], 0.0)
    b.stage(vector=wstm)
    for k in range(len(tjobs) + 1):
        def sload(w, k=k):
            nm, i, cols, idx = tjobs[k]
            w.dma(wld[k % 2][:, 0:cols], ins[nm][i*128:(i+1)*128, :])
        def sabs(w, k=k):
            nm, i, cols, idx = tjobs[k-1]
            buf = wld[(k-1) % 2]
            w.activation(out=buf[:, 0:cols], in_=buf[:, 0:cols], func=AF.Abs)
            w.ev(f"wsab_{k-1}")
        def scomp(w, k=k):
            nm, i, cols, idx = tjobs[k-1]
            buf = wld[(k-1) % 2]
            w.wait(f"wsab_{k-1}")
            w.reduce_sum(wstat[:, k-1:k], buf[:, 0:cols], axis=AX.X)
            w.reduce_max(wstat[:, 32+k-1:32+k], buf[:, 0:cols], axis=AX.X)
        kw = {}
        if k < len(tjobs):
            kw["sync"] = sload
        if k > 0:
            kw["scalar"] = sabs
            kw["vector"] = scomp
        b.stage(**kw)
    onesfc = b.sb("onesfc", [128, 1], F32)
    wmx_d = b.dram("wmx_d", [32, 2], F32)
    def wst0(w):
        w.memset(onesfc[:, 0:1], 1.0)
    b.stage(vector=wst0)
    def wst1(w):
        w.matmul(psf[0][0:1, 0:64], onesfc[:, 0:1], wstat[:, 0:64],
                 start=True, stop=True)
        w.ev("l0wst")
    def wst2(w):
        w.wait("l0wst")
        w.tensor_copy(srow[0:1, 64:128], psf[0][0:1, 0:64])
        ofs = 0
        for ji, (nm, rows, cols, idx) in enumerate(jobs):
            nt = rows // 128
            w.reduce_sum(srow[0:1, 200+idx:201+idx], srow[0:1, 64+ofs:64+ofs+nt],
                         axis=AX.X)
            ofs += nt
        w.reduce_max(wstat[:, 40:41], wstat[:, 32+20:32+28], axis=AX.X)
        w.reduce_max(wstat[:, 41:42], wstat[:, 32+28:32+30], axis=AX.X)
        w.tensor_copy(wstat[0:64, 44:46], wstat[64:128, 40:42])
        w.tensor_max(wstat[0:64, 40:42], wstat[0:64, 40:42], wstat[0:64, 44:46])
        w.tensor_copy(wstat[0:32, 44:46], wstat[32:64, 40:42])
        w.tensor_max(wstat[0:32, 40:42], wstat[0:32, 40:42], wstat[0:32, 44:46])
    b.stage(tensor=wst1, vector=wst2)
    def wst3(w):
        w.dma(wmx_d, wstat[0:32, 40:42])
        w.wait_dma()
        w.dma(srow[0:1, 128:192].rearrange("a (x y) -> a x y", x=1)[:, 0, :],
              wmx_d.rearrange("a b -> (a b)").rearrange("(x y) -> x y", x=1))
    b.stage(gpsimd=wst3)
    def wst4(w):
        w.reduce_max(srow[0:1, 206:207],
                     srow[0:1, 128:192].rearrange("a (p t) -> a t p", t=2)[:, 0:1, :],
                     axis=AX.X)
    b.stage(vector=wst4)
    def wst4b(w):
        w.reduce_max(srow[0:1, 207:208],
                     srow[0:1, 128:192].rearrange("a (p t) -> a t p", t=2)[:, 1:2, :],
                     axis=AX.X)
    b.stage(vector=wst4b)
    def wst5(w):
        w.dma(stat_out[0:1, 0:6], srow[0:1, 200:206])
        w.dma(stat_out[0:1, 6:8], srow[0:1, 206:208])
        w.wait_dma()
    b.stage(gpsimd=wst5)
    aa_t = wld[0][0:16, 6144:8192]
    def pBA(w):
        w.dma(bb_t[0:16, 0:E], ba_d[0:16, :])
        w.dma(aa_t[:, 0:E], ba_d[16:32, :])
    b.stage(sync=pBA)
    C = dict(psf=psf, wb=None, srow=srow, onesf=onesf, onesbf=onesbc,
             conv_d=conv_d, pre_d=pre_d, z_d=None, g_d=g_d, gc_d=gc_d,
             lk_d=lk_d, lq_d=lq_d, tab_d=tab_d, ssc_d=ssc_d,
             bb_t=bb_t, aa_t=aa_t, cacc=cacc, sq2=sq2,
             EAf=lk16t, B1f=lq16t, gated_out=None, ident=None, psb=None)
    build_l0a_body2(b, ins, C)
    build_l0a_body3(b, ins, C)
    return b

def build_l0a_body(b, ins, C):
    xn, psf, wb = C["xn"], C["psf"], C["wb"]
    pre_d, z_d = C["pre_d"], C["z_d"]
    prh = [b.sb("prh0", [128, E], F32)]
    def evP(mi, ti, pp, w, mw, tw):
        w.tensor_copy(prh[0][:, ti*512:ti*512+tw], pp[0:mw, 0:tw])
    def postP(w, mi):
        w.dma(pre_d[mi*128:(mi+1)*128, :], prh[0][:, 0:E])
    KO.mm_stream(b, ins["wqkvT"], E, 4096, xn, E, psf, wb, evP, post=postP)
    zrh = [b.sb("zrh0", [128, E], BF16)]
    def evZ(mi, ti, pp, w, mw, tw):
        w.activation(out=zrh[0][:, ti*512:ti*512+tw], in_=pp[0:mw, 0:tw],
                     func=AF.Silu)
    def postZ(w, mi):
        w.dma(z_d[mi*128:(mi+1)*128, :], zrh[0][:, 0:E])
    KO.mm_stream(b, ins["wzT"], E, E, xn, E, psf, wb, evZ, evac_eng="scalar",
                 post=postZ)
    bb_t = b.sb("bb_t", [16, E], F32)
    aa_t = b.sb("aa_t", [16, E], F32)
    def evB(mi, ti, pp, w, mw, tw):
        w.activation(out=bb_t[0:16, ti*512:ti*512+tw], in_=pp[0:mw, 0:tw],
                     func=AF.Sigmoid)
    KO.mm_stream(b, ins["wbT"], E, 16, xn, E, psf, wb, evB, evac_eng="scalar")
    def evA(mi, ti, pp, w, mw, tw):
        w.tensor_copy(aa_t[:, ti*512:ti*512+tw], pp[0:mw, 0:tw])
    KO.mm_stream(b, ins["waT"], E, 16, xn, E, psf, wb, evA)
    C["bb_t"], C["aa_t"] = bb_t, aa_t
    return b

def build_l0a_body2(b, ins, C):
    E_ = E
    psf, wb, srow, onesf, onesbf = C["psf"], C["wb"], C["srow"], C["onesf"], C["onesbf"]
    conv_d, pre_d = C["conv_d"], C["pre_d"]
    g_d, gc_d, tab_d, ssc_d = C["g_d"], C["gc_d"], C["tab_d"], C["ssc_d"]
    bb_t, aa_t = C["bb_t"], C["aa_t"]
    cw = b.sb("cw", [128, 128], F32)
    cb = b.sb("cb", [128, 32], F32)
    def pc0(w):
        w.dma(cw[:], ins["convw"])
        w.dma(cb[:], ins["convb"])
    b.stage(sync=pc0)
    cbuf = [[b.sb(f"cb{i}", [128, E], F32) for i in range(2)]]
    cobuf = [[b.sb(f"co{i}", [128, E], F32) for i in range(2)]]
    cacc = C["cacc"]
    def cfn(w, k, tins, touts):
        pre = tins[0]
        w.tensor_scalar(out=cacc[:, 0:E], in0=pre[:, 0:E],
                        scalar1=cw[:, k*4+3:k*4+4], scalar2=cb[:, k:k+1],
                        op0=ALU.mult, op1=ALU.add)
        for j in range(3):
            sh = 3 - j
            w.scalar_tensor_tensor(out=cacc[:, sh:E], in0=pre[:, 0:E-sh],
                                   scalar=cw[:, k*4+j:k*4+j+1], in1=cacc[:, sh:E],
                                   op0=ALU.mult, op1=ALU.add)
        w.ev(f"cvacc_{k}")
    for k in range(32 + 2):
        def sload(w, k=k):
            if k < 32:
                w.dma(cbuf[0][k % 2][:, 0:E], pre_d[k*128:(k+1)*128, :])
        def scomp(w, k=k):
            if 0 < k <= 32:
                cfn(w, k - 1, [cbuf[0][(k-1) % 2]], None)
        def ssil(w, k=k):
            if 0 < k <= 32:
                w.wait(f"cvacc_{k-1}")
                w.activation(out=cobuf[0][(k-1) % 2][:, 0:E], in_=cacc[:, 0:E],
                             func=AF.Silu)
                w.ev(f"cvs_{k-1}")
        def sstore(w, k=k):
            if 1 < k <= 33:
                w.wait(f"cvs_{k-2}")
                w.dma(conv_d[(k-2)*128:(k-1)*128, :], cobuf[0][k % 2][:, 0:E])
        kw = {}
        if k < 32:
            kw["sync"] = sload
        if 0 < k <= 32:
            kw["vector"] = scomp
            kw["scalar"] = ssil
        if 1 < k <= 33:
            kw["gpsimd"] = sstore
        b.stage(**kw)
    wld0 = b._l0ctx_wld[0]
    g16 = wld0[0:16, 2048:4096]
    t16 = wld0[0:16, 4096:6144]
    al = b.sb("al", [16, 2], F32)
    def pg0(w):
        w.dma(al[0:16, 0:1], ins["alog"])
        w.dma(al[0:16, 1:2], ins["dtb"])
    b.stage(sync=pg0)
    def pg1(w):
        w.tensor_scalar(out=g16[:, 0:E], in0=aa_t[:, 0:E], scalar1=al[0:16, 1:2],
                        scalar2=None, op0=ALU.add)
    b.stage(vector=pg1)
    def pg2(w):
        w.activation(out=t16[:, 0:E], in_=g16[:, 0:E], func=AF.Abs)
    b.stage(scalar=pg2)
    def pg3(w):
        w.activation(out=t16[:, 0:E], in_=t16[:, 0:E], func=AF.Exp, scale=-1.0)
    b.stage(scalar=pg3)
    def pg4(w):
        w.activation(out=t16[:, 0:E], in_=t16[:, 0:E], func=AF.Ln, bias=1.0)
    b.stage(scalar=pg4)
    def pg5(w):
        w.tensor_scalar(out=g16[:, 0:E], in0=g16[:, 0:E], scalar1=0.0,
                        scalar2=None, op0=ALU.max)
        w.tensor_add(g16[:, 0:E], g16[:, 0:E], t16[:, 0:E])
    b.stage(vector=pg5)
    def pg6(w):
        w.activation(out=al[0:16, 0:1], in_=al[0:16, 0:1], func=AF.Exp)
    b.stage(scalar=pg6)
    def pg7(w):
        w.tensor_scalar(out=al[0:16, 0:1], in0=al[0:16, 0:1], scalar1=-1.0,
                        scalar2=None, op0=ALU.mult)
    b.stage(vector=pg7)
    def pg8(w):
        w.tensor_scalar(out=g16[:, 0:E], in0=g16[:, 0:E], scalar1=al[0:16, 0:1],
                        scalar2=None, op0=ALU.mult)
        w.ev("l0graw")
    b.stage(vector=pg8)
    def pg9(w):
        w.dma(g_d[0:16, :], g16[:, 0:E])
        w.wait_dma()
    b.stage(gpsimd=pg9)
    C["g16"], C["g16b"] = g16, t16
    sc4 = [b.sb(f"sc4{i}", [128, 64], F32) for i in range(2)]
    zz = b.sb("zz64", [128, 64], F32)
    def pgA(w):
        w.memset(zz[:, 0:64], 0.0)
    b.stage(vector=pgA)
    for m in range(4 + 2):
        def sld(w, m=m):
            if m < 4:
                w.dma(sc4[m % 2][:, 0:64],
                      g_d.rearrange("v (c s) -> (v c) s", s=64)[m*128:(m+1)*128, :])
        def scm(w, m=m):
            if 0 < m <= 4:
                w.tensor_tensor_scan(out=sc4[(m-1) % 2][:, 0:64],
                                     data0=sc4[(m-1) % 2][:, 0:64],
                                     data1=zz[:, 0:64], initial=0.0,
                                     op0=ALU.add, op1=ALU.add)
                w.ev(f"l0gc_{m-1}")
        def sst(w, m=m):
            if 0 < m <= 4:
                w.wait(f"l0gc_{m-1}")
                w.dma(gc_d.rearrange("v (c s) -> (v c) s", s=64)[(m-1)*128:m*128, :],
                      sc4[(m-1) % 2][:, 0:64])
        kw = {}
        if m < 4:
            kw["sync"] = sld
        if 0 < m <= 4:
            kw["vector"] = scm
            kw["gpsimd"] = sst
        b.stage(**kw)
    return b

def build_l0a_body3(b, ins, C):
    psf, psb, wb = C["psf"], C["psb"], C["wb"]
    srow, onesf, onesbf, ident = C["srow"], C["onesf"], C["onesbf"], C["ident"]
    conv_d, z_d, g_d, gc_d = C["conv_d"], C["z_d"], C["g_d"], C["gc_d"]
    lk_d, lq_d, tab_d, ssc_d = C["lk_d"], C["lq_d"], C["tab_d"], C["ssc_d"]
    bb_t, g16, t16 = C["bb_t"], C["g16"], C["g16b"]
    gated_out = C["gated_out"]
    import numpy as np
    LNS = float(np.log(128.0 ** -0.5))
    kfm = b.sb("kfm", [128, E], F32)
    qfm = b.sb("qfm", [128, E], F32)
    C["kfm"], C["qfm"] = kfm, qfm
    sqt = C["sq2"]
    for kk in range(8):
        def l0(w, kk=kk):
            w.dma(qfm[:, 0:E], conv_d[kk*128:(kk+1)*128, :])
            w.dma(kfm[:, 0:E], conv_d[1024 + kk*128:1024 + (kk+1)*128, :])
        b.stage(sync=l0)
        def l1(w, kk=kk):
            w.activation(out=sqt[0][:, 0:E], in_=qfm[:, 0:E], func=AF.Square)
            w.activation(out=sqt[1][:, 0:E], in_=kfm[:, 0:E], func=AF.Square)
            w.ev(f"l0ss_{kk}")
        def l2(w, kk=kk):
            w.wait(f"l0ss_{kk}")
            for j in range(4):
                w.matmul(psf[j][0:1, :], onesbf[:, 0:1],
                         sqt[0][:, j*512:(j+1)*512], start=True, stop=True)
            w.ev(f"l0sm_{kk}")
        b.stage(tensor=l2, scalar=l1)
        def l3(w, kk=kk):
            w.wait(f"l0sm_{kk}")
            for j in range(4):
                w.tensor_copy(srow[0:1, j*512:(j+1)*512], psf[j][0:1, :])
        b.stage(vector=l3)
        def l4(w, kk=kk):
            for j in range(4):
                w.matmul(psf[j][0:1, :], onesbf[:, 0:1],
                         sqt[1][:, j*512:(j+1)*512], start=True, stop=True)
            w.ev(f"l0sm2_{kk}")
        def l5(w, kk=kk):
            w.wait(f"l0sm2_{kk}")
            for j in range(4):
                w.tensor_copy(srow[0:1, E + j*512:E + (j+1)*512], psf[j][0:1, :])
        b.stage(tensor=l4, vector=l5)
        def l6(w, kk=kk):
            w.activation(out=srow[0:1, 0:E], in_=srow[0:1, 0:E], func=AF.Ln,
                         bias=1e-6)
            w.activation(out=srow[0:1, E:2*E], in_=srow[0:1, E:2*E], func=AF.Ln,
                         bias=1e-6)
        b.stage(scalar=l6)
        def l7(w, kk=kk):
            w.tensor_scalar(out=srow[0:1, 0:E], in0=srow[0:1, 0:E], scalar1=-0.5,
                            scalar2=None, op0=ALU.mult)
            w.tensor_scalar(out=srow[0:1, E:2*E], in0=srow[0:1, E:2*E],
                            scalar1=-0.5, scalar2=None, op0=ALU.mult)
        b.stage(vector=l7)
        def l8(w, kk=kk):
            w.dma(lq_d[kk:kk+1, :], srow[0:1, 0:E])
            w.dma(lk_d[kk:kk+1, :], srow[0:1, E:2*E])
            w.wait_dma()
        b.stage(gpsimd=l8)
    lk16 = C["EAf"][0:16, 0:E]
    lq16 = C["B1f"][0:16, 0:E]
    gc16 = C["cacc"][0:16, 0:E]
    tt16 = b._l0ctx_wld[0][0:16, 0:2048]
    glr = b.sb("glr", [16, 32], F32)
    def t0(w):
        for kkx in range(8):
            w.dma(lk16[2*kkx:2*kkx+1, 0:E], lk_d[kkx:kkx+1, :])
            w.dma(lk16[2*kkx+1:2*kkx+2, 0:E], lk_d[kkx:kkx+1, :])
            w.dma(lq16[2*kkx:2*kkx+1, 0:E], lq_d[kkx:kkx+1, :])
            w.dma(lq16[2*kkx+1:2*kkx+2, 0:E], lq_d[kkx:kkx+1, :])
        w.dma(gc16[:, :], gc_d[0:16, :])
    b.stage(sync=t0)
    def t1(w):
        w.tensor_add(tt16[:, :], g16[0:16, :], lk16[:, :])
        w.ev("l0t1")
    b.stage(vector=t1)
    def t2(w):
        w.activation(out=tt16[:, :], in_=tt16[:, :], func=AF.Exp)
    b.stage(scalar=t2)
    def t3(w):
        w.tensor_mul(tt16[:, :], tt16[:, :], bb_t[0:16, :])
        w.ev("l0t3")
    b.stage(vector=t3)
    def t4(w):
        w.wait("l0t3")
        w.dma(tab_d["kbeg"][0:16, :], tt16[:, :])
        w.wait_dma()
    b.stage(gpsimd=t4)
    def t5(w):
        w.tensor_copy(glr[0:16, 0:32],
                      g16[0:16, :].rearrange("p (c s) -> p c s", s=64)[:, :, 63:64]
                      .rearrange("p c o -> p (c o)"))
        for c in range(32):
            w.tensor_scalar(out=tt16[:, c*64:(c+1)*64],
                            in0=g16[0:16, c*64:(c+1)*64],
                            scalar1=glr[0:16, c:c+1], scalar2=-1.0,
                            op0=ALU.subtract, op1=ALU.mult)
        w.tensor_add(tt16[:, :], tt16[:, :], lk16[:, :])
    b.stage(vector=t5)
    def t6(w):
        w.activation(out=tt16[:, :], in_=tt16[:, :], func=AF.Exp)
        w.activation(out=glr[0:16, 0:32], in_=glr[0:16, 0:32], func=AF.Exp)
        w.ev("l0t6")
    b.stage(scalar=t6)
    def t7(w):
        w.wait("l0t6")
        w.dma(tab_d["kwsc"][0:16, :], tt16[:, :])
        w.dma(ssc_d[0:16, :], glr[0:16, 0:32])
        w.wait_dma()
    b.stage(gpsimd=t7)
    def t8(w):
        w.tensor_add(tt16[:, :], g16[0:16, :], lq16[:, :])
    b.stage(vector=t8)
    def t9(w):
        w.activation(out=tt16[:, :], in_=tt16[:, :], func=AF.Exp, bias=LNS)
        w.ev("l0t9")
    b.stage(scalar=t9)
    def t10(w):
        w.wait("l0t9")
        w.dma(tab_d["rqe"][0:16, :], tt16[:, :])
        w.wait_dma()
    b.stage(gpsimd=t10)
    def t10b(w):
        w.activation(out=tt16[:, :], in_=lq16[:, :], func=AF.Exp, bias=LNS)
        w.ev("l0t10b")
    b.stage(scalar=t10b)
    def t10c(w):
        w.wait("l0t10b")
        w.dma(tab_d["rq"][0:16, :], tt16[:, :])
        w.wait_dma()
    b.stage(gpsimd=t10c)
    def t11(w):
        w.tensor_add(tt16[:, :], gc16[:, :], lk16[:, :])
        w.ev("l0t11a")
        w.tensor_sub(lk16[:, :], lk16[:, :], gc16[:, :])
        w.ev("l0t11c")
    b.stage(vector=t11)
    def t12(w):
        w.wait("l0t11c")
        w.dma(tab_d["gclk"][0:16, :], tt16[:, :])
        w.dma(tab_d["lkmgc"][0:16, :], lk16[:, :])
        w.dma(tab_d["beta"][0:16, :], bb_t[0:16, :])
        w.dma(tab_d["gc"][0:16, :], gc16[:, :])
        w.wait_dma()
    b.stage(gpsimd=t12)
    return b

def build_l0a_body4(b, ins, C, kks=range(8), gofs=0):
    psf, psb32, psbb = C["psf"], C["psb32"], C["psbb"]
    srow, onesf, ident = C["srow"], C["onesf"], C["ident"]
    identf = C["identf"]
    conv_d, z_d, tab_d, ssc_d = C["conv_d"], C["z_d"], C["tab_d"], C["ssc_d"]
    gated_out = C["gated_out"]
    kfm, qfm = C["kfm"], C["qfm"]
    v_tm, kw_tm, kbeg_tm, kcd = C["v_tm"], C["kw_tm"], C["kbeg_tm"], C["kcd"]
    vi, a2T, core = C["vi"], C["a2T"], C["core"]
    Apow, Bpow, Xc = C["Apow"], C["Bpow"], C["Xc"]
    expE, vfm, zt = C["expE"], C["vfm"], C["zt"]
    EAf = C["EAf"]
    B1f = C["B1f"]
    col2 = [b.sb(f"col2_{v}", [128, 128], F32) for v in range(2)]
    stg = b.sb("stg", [16, 17 * 128], F32)
    sscb = b.sb("sscb", [128, 64], F32)
    S0 = [b.sb(f"S0_{v}", [128, 128], F32) for v in range(2)]
    vnew = [b.sb(f"vnew_{v}", [128, 128], F32) for v in range(2)]
    rstdc = b.sb("rstdc", [128, 16], F32)
    mtrilS = b.sb("mtrilS_t", [128, 128], F32)
    mtriuI = b.sb("mtriuI_t", [128, 128], F32)
    nwb = b.sb("nwb", [128, 128], BF16)
    KINDS = ["kbeg", "kwsc", "beta", "rqe", "gclk", "lkmgc", "gc", "rq"]
    KI = {nm: i for i, nm in enumerate(KINDS)}

    def c2(v, nm, i, bs=0, n=128):
        return col2[v][bs:bs+n, KI[nm]*16 + i:KI[nm]*16 + i + 1]

    def pc(w):
        w.dma(mtrilS[:, 0:128], ins["mtrilSw"])
        w.dma(mtriuI[:, 0:128], ins["mtriuIw"])
        w.dma(nwb[:], ins["nwbc"])
    b.stage(sync=pc)

    for kk in kks:
        vg = [2 * kk, 2 * kk + 1]
        def a0(w, kk=kk, vg=vg):
            w.dma(qfm[:, 0:E], conv_d[kk*128:(kk+1)*128, :])
            w.dma(kfm[:, 0:E], conv_d[1024 + kk*128:1024 + (kk+1)*128, :])
            for v in range(2):
                for ki, nm in enumerate(KINDS):
                    w.dma(stg[0:16, (v*8+ki)*128:(v*8+ki+1)*128],
                          tab_d[nm][vg[v]:vg[v]+1, :]
                          .rearrange("a (c p) -> (a c) p", p=128))
            w.dma(stg[0:1, 16*128:16*128+32], ssc_d[vg[0]:vg[0]+1, :])
            w.dma(stg[0:1, 16*128+32:16*128+64], ssc_d[vg[1]:vg[1]+1, :])
        b.stage(sync=a0)
        def a1(w, kk=kk):
            for gg in range(4):
                if gg >= 1:
                    w.wait(f"a1e_{kk}_{gg-1}")
                for u in range(gg*4, min(16, (gg+1)*4)):
                    w.transpose(psf[u % 4][0:128, 0:16], stg[0:16, u*128:(u+1)*128],
                                identf[0:16, 0:16])
                w.ev(f"a1t_{kk}_{gg}")
        def a1e(w, kk=kk):
            for gg in range(4):
                w.wait(f"a1t_{kk}_{gg}")
                for u in range(gg*4, min(16, (gg+1)*4)):
                    v, ki = u // 8, u % 8
                    w.tensor_copy(col2[v][:, ki*16:(ki+1)*16], psf[u % 4][0:128, 0:16])
                w.ev(f"a1e_{kk}_{gg}")
        b.stage(tensor=a1, vector=a1e)
        def a2(w, kk=kk):
            for v in range(2):
                w.matmul(psb32[:, v*32:(v+1)*32], onesf[0:1, 0:128],
                         stg[0:1, 16*128+v*32:16*128+(v+1)*32], start=True, stop=True)
            w.ev(f"a2_{kk}")
        def a2e(w, kk=kk):
            w.wait(f"a2_{kk}")
            w.tensor_copy(sscb[:, 0:64], psb32[:, 0:64])
            for v in range(2):
                w.memset(S0[v][:, 0:128], 0.0)
        b.stage(tensor=a2, vector=a2e)
        jobs_t = [("k", None), ("v", 0), ("v", 1)]
        for src, vv in jobs_t:
            if src == "v":
                def tload(w, kk=kk, vv=vv, vg=vg):
                    w.dma(vfm[:, 0:E],
                          conv_d[2048 + vg[vv]*128:2048 + (vg[vv]+1)*128, :])
                b.stage(sync=tload)
            stt = kfm if src == "k" else vfm
            for g in range(2):
                def tt_(w, kk=kk, src=src, g=g, stt=stt, vv=vv):
                    if g == 1:
                        w.wait(f"te_{kk}_{src}_{vv}_0")
                    for i in range(g*8, (g+1)*8):
                        w.transpose(psb32[:, (i - g*8)*128:(i - g*8 + 1)*128],
                                    stt[:, i*128:(i+1)*128], identf[:, 0:128])
                    w.ev(f"tt_{kk}_{src}_{vv}_{g}")
                def te_(w, kk=kk, src=src, g=g, vv=vv):
                    w.wait(f"tt_{kk}_{src}_{vv}_{g}")
                    for i in range(g*8, (g+1)*8):
                        psl = psb32[:, (i - g*8)*128:(i - g*8 + 1)*128]
                        if src == "k":
                            for v in range(2):
                                w.tensor_scalar(out=kbeg_tm[v][:, i*128:(i+1)*128],
                                                in0=psl, scalar1=c2(v, "kbeg", i),
                                                scalar2=None, op0=ALU.mult)
                                w.tensor_scalar(out=kw_tm[v][:, i*128:(i+1)*128],
                                                in0=psl, scalar1=c2(v, "kwsc", i),
                                                scalar2=None, op0=ALU.mult)
                        else:
                            w.tensor_scalar(out=v_tm[vv][:, i*128:(i+1)*128],
                                            in0=psl, scalar1=c2(vv, "beta", i),
                                            scalar2=None, op0=ALU.mult)
                    w.ev(f"te_{kk}_{src}_{vv}_{g}")
                b.stage(tensor=tt_, vector=te_)
        for v in range(2):
            def i0(w, kk=kk, v=v, vg=vg):
                w.dma(srow[0:1, 0:E], tab_d["lkmgc"][vg[v]:vg[v]+1, :])
            b.stage(sync=i0)
            def i1(w, kk=kk, v=v):
                for j in range(4):
                    w.matmul(psf[j][:, :], onesf[0:1, 0:128],
                             srow[0:1, j*512:(j+1)*512], start=True, stop=True)
                    w.ev(f"i1_{kk}_{v}_{j}")
            def i1e(w, kk=kk, v=v):
                for j in range(4):
                    w.wait(f"i1_{kk}_{v}_{j}")
                    w.tensor_copy(B1f[:, j*512:(j+1)*512], psf[j][:, :])
            b.stage(tensor=i1, vector=i1e)
            def i2(w, kk=kk, v=v):
                for i in range(16):
                    w.tensor_scalar(out=EAf[:, i*128:(i+1)*128],
                                    in0=B1f[:, i*128:(i+1)*128],
                                    scalar1=c2(v, "gclk", i), scalar2=30.0,
                                    op0=ALU.add, op1=ALU.min)
            b.stage(vector=i2)
            def i3(w, kk=kk, v=v):
                w.activation(out=expE[:, 0:E], in_=EAf[:, 0:E], func=AF.Exp)
            b.stage(scalar=i3)
            def i4(w, kk=kk, v=v):
                for i in range(16):
                    w.matmul(psf[i // 4][:, (i % 4)*128:(i % 4 + 1)*128],
                             kfm[:, i*128:(i+1)*128], kfm[:, i*128:(i+1)*128],
                             start=True, stop=True)
                w.ev(f"i4_{kk}_{v}")
            def i4e(w, kk=kk, v=v):
                w.wait(f"i4_{kk}_{v}")
                for i in range(16):
                    psl = psf[i // 4][:, (i % 4)*128:(i % 4 + 1)*128]
                    w.tensor_tensor(out=EAf[:, i*128:(i+1)*128], in0=psl,
                                    in1=expE[:, i*128:(i+1)*128], op=ALU.mult)
                for i in range(16):
                    w.scalar_tensor_tensor(out=Apow[0][:, i*128:(i+1)*128],
                                           in0=EAf[:, i*128:(i+1)*128],
                                           scalar=c2(v, "beta", i),
                                           in1=mtrilS[:, 0:128],
                                           op0=ALU.mult, op1=ALU.mult)
                w.ev(f"i4t_{kk}_{v}")
            def i4g(w, kk=kk, v=v):
                w.wait(f"i4t_{kk}_{v}")
                for i in range(16):
                    w.tensor_add(Xc[0][:, i*128:(i+1)*128],
                                 Apow[0][:, i*128:(i+1)*128], identf[:, 0:128])
            b.stage(tensor=i4, vector=i4e, gpsimd=i4g)
            for g in range(2):
                def i5(w, kk=kk, g=g, v=v):
                    if g == 1:
                        w.wait(f"i5e_{kk}_{v}_0")
                    for i in range(g*8, (g+1)*8):
                        w.transpose(psb32[:, (i - g*8)*128:(i - g*8 + 1)*128],
                                    Apow[0][:, i*128:(i+1)*128], identf[:, 0:128])
                    w.ev(f"i5_{kk}_{v}_{g}")
                def i5e(w, kk=kk, g=g, v=v):
                    w.wait(f"i5_{kk}_{v}_{g}")
                    w.tensor_copy(Bpow[0][:, g*1024:(g+1)*1024], psb32[:, 0:1024])
                    w.ev(f"i5e_{kk}_{v}_{g}")
                b.stage(tensor=i5, vector=i5e)
            cc = 0
            for lev in range(5):
                last = (lev == 4)
                def cpv(dst, tag):
                    def fn(w, dst=dst, tag=tag):
                        w.wait(tag)
                        for j in range(2):
                            w.tensor_copy(dst[:, j*512:(j+1)*512], psf[j][:, :])
                    return fn
                def cps(dst, tag):
                    def fn(w, dst=dst, tag=tag):
                        w.wait(tag)
                        for j in range(2, 4):
                            w.activation(out=dst[:, j*512:(j+1)*512],
                                         in_=psf[j][:, :], func=AF.Copy)
                    return fn
                if not last:
                    def c1(w, kk=kk, v=v, cc=cc, lev=lev):
                        for i in range(16):
                            w.matmul(psf[i // 4][:, (i % 4)*128:(i % 4 + 1)*128],
                                     Bpow[cc][:, i*128:(i+1)*128],
                                     Apow[cc][:, i*128:(i+1)*128],
                                     start=True, stop=True)
                        w.ev(f"c1_{kk}_{v}_{lev}")
                    b.stage(tensor=c1, vector=cpv(Apow[1-cc], f"c1_{kk}_{v}_{lev}"),
                            scalar=cps(Apow[1-cc], f"c1_{kk}_{v}_{lev}"))
                def c2s(w, kk=kk, v=v, cc=cc, lev=lev):
                    for i in range(16):
                        w.matmul(psf[i // 4][:, (i % 4)*128:(i % 4 + 1)*128],
                                 Apow[cc][:, i*128:(i+1)*128],
                                 Bpow[cc][:, i*128:(i+1)*128],
                                 start=True, stop=True)
                    w.ev(f"c2_{kk}_{v}_{lev}")
                b.stage(tensor=c2s, vector=cpv(Bpow[1-cc], f"c2_{kk}_{v}_{lev}"),
                        scalar=cps(Bpow[1-cc], f"c2_{kk}_{v}_{lev}"))
                def c3(w, kk=kk, v=v, cc=cc, lev=lev):
                    for i in range(16):
                        w.matmul(psf[i // 4][:, (i % 4)*128:(i % 4 + 1)*128],
                                 identf[:, 0:128], Xc[cc][:, i*128:(i+1)*128],
                                 start=True, stop=False)
                        w.matmul(psf[i // 4][:, (i % 4)*128:(i % 4 + 1)*128],
                                 Bpow[1-cc][:, i*128:(i+1)*128],
                                 Xc[cc][:, i*128:(i+1)*128],
                                 start=False, stop=True)
                    w.ev(f"c3_{kk}_{v}_{lev}")
                b.stage(tensor=c3, vector=cpv(Xc[1-cc], f"c3_{kk}_{v}_{lev}"),
                        scalar=cps(Xc[1-cc], f"c3_{kk}_{v}_{lev}"))
                cc = 1 - cc
            for g in range(2):
                def i6(w, kk=kk, v=v, g=g, cc=cc):
                    if g == 1:
                        w.wait(f"i6e_{kk}_{v}_0")
                    for i in range(g*8, (g+1)*8):
                        w.transpose(psb32[:, (i - g*8)*128:(i - g*8 + 1)*128],
                                    Xc[cc][:, i*128:(i+1)*128], identf[:, 0:128])
                    w.ev(f"i6_{kk}_{v}_{g}")
                def i6e(w, kk=kk, v=v, g=g):
                    w.wait(f"i6_{kk}_{v}_{g}")
                    w.tensor_copy(expE[:, g*1024:(g+1)*1024], psb32[:, 0:1024])
                    w.ev(f"i6e_{kk}_{v}_{g}")
                b.stage(tensor=i6, vector=i6e)
            def i7(w, kk=kk, v=v):
                for i in range(16):
                    w.matmul(psf[i // 4][:, (i % 4)*128:(i % 4 + 1)*128],
                             expE[:, i*128:(i+1)*128], v_tm[v][:, i*128:(i+1)*128],
                             start=True, stop=True)
                w.ev(f"i7_{kk}_{v}")
            def i7e(w, kk=kk, v=v):
                w.wait(f"i7_{kk}_{v}")
                for j in range(2):
                    w.tensor_copy(vi[v][:, j*512:(j+1)*512], psf[j][:, :])
            def i7s(w, kk=kk, v=v):
                w.wait(f"i7_{kk}_{v}")
                for j in range(2, 4):
                    w.activation(out=vi[v][:, j*512:(j+1)*512], in_=psf[j][:, :],
                                 func=AF.Copy)
            b.stage(tensor=i7, vector=i7e, scalar=i7s)
            def i8(w, kk=kk, v=v):
                for i in range(16):
                    w.matmul(psf[i // 4][:, (i % 4)*128:(i % 4 + 1)*128],
                             kbeg_tm[v][:, i*128:(i+1)*128], expE[:, i*128:(i+1)*128],
                             start=True, stop=True)
                w.ev(f"i8_{kk}_{v}")
            def i8e(w, kk=kk, v=v):
                w.wait(f"i8_{kk}_{v}")
                for j in range(2):
                    w.tensor_copy(kcd[v][:, j*512:(j+1)*512], psf[j][:, :])
            def i8s(w, kk=kk, v=v):
                w.wait(f"i8_{kk}_{v}")
                for j in range(2, 4):
                    w.activation(out=kcd[v][:, j*512:(j+1)*512], in_=psf[j][:, :],
                                 func=AF.Copy)
            b.stage(tensor=i8, vector=i8e, scalar=i8s)
            def i9pre(w, kk=kk, v=v, vg=vg):
                w.dma(srow[0:1, 0:E], tab_d["gc"][vg[v]:vg[v]+1, :])
            b.stage(sync=i9pre)
            def i9b(w, kk=kk, v=v):
                for j in range(4):
                    w.matmul(psf[j][:, :], onesf[0:1, 0:128],
                             srow[0:1, j*512:(j+1)*512], start=True, stop=True)
                    w.ev(f"i9b_{kk}_{v}_{j}")
            def i9be(w, kk=kk, v=v):
                for j in range(4):
                    w.wait(f"i9b_{kk}_{v}_{j}")
                    w.tensor_copy(B1f[:, j*512:(j+1)*512], psf[j][:, :])
            b.stage(tensor=i9b, vector=i9be)
            def i9(w, kk=kk, v=v):
                for i in range(16):
                    w.tensor_scalar(out=EAf[:, i*128:(i+1)*128],
                                    in0=B1f[:, i*128:(i+1)*128],
                                    scalar1=c2(v, "lkmgc", i), scalar2=30.0,
                                    op0=ALU.add, op1=ALU.min)
            b.stage(vector=i9)
            def i10(w, kk=kk, v=v):
                w.activation(out=expE[:, 0:E], in_=EAf[:, 0:E], func=AF.Exp)
            b.stage(scalar=i10)
            def i11(w, kk=kk, v=v):
                for i in range(16):
                    w.matmul(psf[i // 4][:, (i % 4)*128:(i % 4 + 1)*128],
                             kfm[:, i*128:(i+1)*128], qfm[:, i*128:(i+1)*128],
                             start=True, stop=True)
                w.ev(f"i11_{kk}_{v}")
            def i11e(w, kk=kk, v=v):
                w.wait(f"i11_{kk}_{v}")
                for i in range(16):
                    psl = psf[i // 4][:, (i % 4)*128:(i % 4 + 1)*128]
                    w.tensor_tensor(out=EAf[:, i*128:(i+1)*128], in0=psl,
                                    in1=expE[:, i*128:(i+1)*128], op=ALU.mult)
                w.ev(f"i11t_{kk}_{v}")
                for i in range(8):
                    w.tensor_tensor(out=a2T[v][:, i*128:(i+1)*128],
                                    in0=EAf[:, i*128:(i+1)*128],
                                    in1=mtriuI[:, 0:128], op=ALU.mult)
            def i11g(w, kk=kk, v=v):
                w.wait(f"i11t_{kk}_{v}")
                for i in range(8, 16):
                    w.tensor_tensor(out=a2T[v][:, i*128:(i+1)*128],
                                    in0=EAf[:, i*128:(i+1)*128],
                                    in1=mtriuI[:, 0:128], op=ALU.mult)
            b.stage(tensor=i11, vector=i11e, gpsimd=i11g)
        import os
        if os.environ.get("L0DBG") and kk == int(os.environ["L0DBG"]):
            dbgs = {}
            for nm, t in [("vi0", vi[0]), ("kcd0", kcd[0]), ("a2T0", a2T[0]),
                          ("vi1", vi[1]), ("kcd1", kcd[1]), ("a2T1", a2T[1]),
                          ("kbeg0", kbeg_tm[0]), ("kw0", kw_tm[0]),
                          ("vtm0", v_tm[0]), ("kbeg1", kbeg_tm[1]),
                          ("kw1", kw_tm[1]), ("vtm1", v_tm[1])]:
                d = b.dram(f"dbg_{nm}", [128, E], F32, kind="ExternalOutput")
                dbgs[nm] = (d, t)
            dcol = b.dram("dbg_col2", [128, 256], F32, kind="ExternalOutput")
            def dmp(w, dbgs=dbgs):
                for nm, (d, t) in dbgs.items():
                    w.dma(d, t[:, 0:E])
                w.dma(dcol[:, 0:128], col2[0][:, 0:128])
                w.dma(dcol[:, 128:256], col2[1][:, 0:128])
                w.wait_dma()
            b.stage(gpsimd=dmp)
        for c in range(32):
            i, bs = c // 2, (c % 2) * 64
            def s12(w, kk=kk, c=c, i=i, bs=bs):
                for v in range(2):
                    w.matmul(psf[v][bs:bs+64, 0:128],
                             kcd[v][:, c*64:(c+1)*64], S0[v][:, 0:128],
                             start=True, stop=True)
                    w.matmul(psf[2+v][bs:bs+64, 0:128],
                             qfm[:, c*64:(c+1)*64], S0[v][:, 0:128],
                             start=True, stop=True)
                w.ev(f"s1_{kk}_{c}")
                w.wait(f"s1e_{kk}_{c}")
                for v in range(2):
                    w.matmul(psf[v][bs:bs+64, 0:128],
                             a2T[v][bs:bs+64, i*128+bs:i*128+bs+64],
                             vnew[v][bs:bs+64, 0:128], start=True, stop=True)
                    w.matmul(psb32[:, v*128:(v+1)*128],
                             kw_tm[v][bs:bs+64, i*128:(i+1)*128],
                             vnew[v][bs:bs+64, 0:128], start=True, stop=True)
                w.ev(f"s2_{kk}_{c}")
            def s12e(w, kk=kk, c=c, i=i, bs=bs, vg=vg):
                w.wait(f"s1_{kk}_{c}")
                for v in range(2):
                    w.tensor_sub(vnew[v][bs:bs+64, 0:128],
                                 vi[v][bs:bs+64, i*128:(i+1)*128],
                                 psf[v][bs:bs+64, 0:128])
                w.ev(f"s1e_{kk}_{c}")
                for v in range(2):
                    w.tensor_scalar(out=core[v][bs:bs+64, i*128:(i+1)*128],
                                    in0=psf[2+v][bs:bs+64, 0:128],
                                    scalar1=c2(v, "rqe", i, bs, 64), scalar2=None,
                                    op0=ALU.mult)
                w.wait(f"s2_{kk}_{c}")
                for v in range(2):
                    w.scalar_tensor_tensor(out=core[v][bs:bs+64, i*128:(i+1)*128],
                                           in0=psf[v][bs:bs+64, 0:128],
                                           scalar=c2(v, "rq", i, bs, 64),
                                           in1=core[v][bs:bs+64, i*128:(i+1)*128],
                                           op0=ALU.mult, op1=ALU.add)
                    w.scalar_tensor_tensor(out=S0[v][:, 0:128], in0=S0[v][:, 0:128],
                                           scalar=sscb[:, v*32+c:v*32+c+1],
                                           in1=psb32[:, v*128:(v+1)*128],
                                           op0=ALU.mult, op1=ALU.add)
            b.stage(tensor=s12, vector=s12e)
        for v in range(2):
            def g0(w, kk=kk, v=v):
                w.activation(out=EAf[:, 0:E], in_=core[v][:, 0:E], func=AF.Square)
            b.stage(scalar=g0)
            def g1(w, kk=kk, v=v):
                w.tensor_reduce(rstdc[:, 0:16],
                                EAf[:, 0:E].rearrange("p (c d) -> p c d", d=128),
                                axis=AX.X, op=ALU.add)
                w.tensor_scalar(out=rstdc[:, 0:16], in0=rstdc[:, 0:16],
                                scalar1=1.0/128.0, scalar2=None, op0=ALU.mult)
            b.stage(vector=g1)
            def g2(w, kk=kk, v=v):
                w.activation(out=rstdc[:, 0:16], in_=rstdc[:, 0:16], func=AF.Sqrt,
                             bias=1e-6)
            b.stage(scalar=g2)
            def g3(w, kk=kk, v=v):
                w.reciprocal(rstdc[:, 0:16], rstdc[:, 0:16])
                for i in range(16):
                    w.scalar_tensor_tensor(out=core[v][:, i*128:(i+1)*128],
                                           in0=core[v][:, i*128:(i+1)*128],
                                           scalar=rstdc[:, i:i+1], in1=nwb[:, 0:128],
                                           op0=ALU.mult, op1=ALU.mult)
            b.stage(vector=g3)
            def g4(w, kk=kk, v=v, vg=vg):
                w.dma(zt[:, 0:E], z_d[vg[v]*128:(vg[v]+1)*128, :])
            b.stage(gpsimd=g4)
            for g in range(2):
                def g5(w, kk=kk, v=v, g=g):
                    if g == 1:
                        w.wait(f"g5e_{kk}_{v}_0")
                    for i in range(g*8, (g+1)*8):
                        w.transpose(psb32[:, (i - g*8)*128:(i - g*8 + 1)*128],
                                    zt[:, i*128:(i+1)*128], identf[:, 0:128])
                    w.ev(f"g5_{kk}_{v}_{g}")
                def g5e(w, kk=kk, v=v, g=g):
                    w.wait(f"g5_{kk}_{v}_{g}")
                    for i in range(g*8, (g+1)*8):
                        w.tensor_mul(core[v][:, i*128:(i+1)*128],
                                     core[v][:, i*128:(i+1)*128],
                                     psb32[:, (i - g*8)*128:(i - g*8 + 1)*128])
                    w.ev(f"g5e_{kk}_{v}_{g}")
                b.stage(tensor=g5, vector=g5e)
            def g6(w, kk=kk, v=v, vg=vg):
                co = vg[v]*128 - gofs
                for i in range(16):
                    w.dma(gated_out[i*128:(i+1)*128, co:co+128],
                          core[v][:, i*128:(i+1)*128])
                w.wait_dma()
            b.stage(gpsimd=g6)
    return b

L0A_TABS = ("kbeg", "kwsc", "rqe", "rq", "gclk", "lkmgc", "beta", "gc")

def build_l0a_p2(kk0):
    b = Builder()
    ins = {}
    for nm, shp, dt in [("nwbc", [128, 128], BF16), ("identD", [128, 128], BF16),
                        ("mtrilSw", [128, 128], F32), ("mtriuIw", [128, 128], F32),
                        ("identDf", [128, 128], F32)]:
        ins[nm] = b.dram(nm, shp, dt, kind="ExternalInput")
    conv_d = b.dram("conv_d", [4096, E], F32, kind="ExternalInput")
    z_d = b.dram("z_d", [E, E], BF16, kind="ExternalInput")
    tab_d = {nm: b.dram(f"tab_{nm}", [16, E], F32, kind="ExternalInput")
             for nm in L0A_TABS}
    ssc_d = b.dram("ssc_d", [16, 32], F32, kind="ExternalInput")
    gated_out = b.dram("gated_out", [E, 1024], BF16, kind="ExternalOutput")
    WF = [b.sb(f"WF{i}", [128, E], F32) for i in range(17)]
    core = [WF[12], WF[13]]
    vfm = b.sb("vfm", [128, E], F32)
    kfm = b.sb("kfm", [128, E], F32)
    qfm = b.sb("qfm", [128, E], F32)
    srow = b.sb("srow", [1, E], F32)
    onesf = b.sb("onesf", [1, 128], F32)
    ident = b.sb("ident", [128, 128], BF16)
    identf = b.sb("identf", [128, 128], F32)
    EAf = b.sb("EAf", [128, E], F32)
    B1f = b.sb("B1f", [128, E], F32)
    psf = [b.ps(f"psf{i}", [128, 512], F32) for i in range(4)]
    psb32 = b.ps("psb32", [128, 1024], F32)
    def p0(w):
        w.dma(ident[:], ins["identD"])
        w.dma(identf[:], ins["identDf"])
    b.stage(sync=p0)
    def p1(w):
        w.memset(onesf[0:1, :], 1.0)
    b.stage(vector=p1)
    C = dict(psf=psf, psb32=psb32, psbb=None, srow=srow, onesf=onesf,
             ident=ident, identf=identf, conv_d=conv_d, z_d=z_d, tab_d=tab_d,
             ssc_d=ssc_d, gated_out=gated_out, kfm=kfm, qfm=qfm,
             EAf=EAf, B1f=B1f, core=core, vfm=vfm, zt=kfm,
             v_tm=WF[0:2], kw_tm=WF[2:4], kbeg_tm=WF[4:6], kcd=WF[6:8],
             vi=WF[8:10], a2T=WF[10:12], Apow=WF[12:14], Bpow=WF[14:16],
             Xc=[WF[16], EAf], expE=B1f)
    assert core[0] is WF[12] and core[1] is WF[13]
    build_l0a_body4(b, ins, C, kks=range(kk0, kk0 + 4), gofs=kk0 * 256)
    return b

def prep_l0a_p2(core, inp, p1res):
    g = lambda n: np.asarray(inp[n], np.float32)
    blkS = np.zeros((128, 128), np.float32)
    blkU = np.zeros((128, 128), np.float32)
    for o in (0, 64):
        blkS[o:o+64, o:o+64] = np.tril(np.full((64, 64), -1.0), -1)
        blkU[o:o+64, o:o+64] = np.triu(np.ones((64, 64)), 0)
    d = {
        "nwbc": to_bf(np.broadcast_to(g("L0_attn_norm_weight"), (128, 128))),
        "identD": np.eye(128, dtype=ml_dtypes.bfloat16),
        "identDf": np.eye(128, dtype=np.float32),
        "mtrilSw": blkS, "mtriuIw": blkU,
        "conv_d": np.asarray(p1res["conv_d"]),
        "z_d": np.asarray(p1res["z_d"]),
        "ssc_d": np.asarray(p1res["ssc_d"]),
    }
    for nm in L0A_TABS:
        d[f"tab_{nm}"] = np.asarray(p1res[f"tab_{nm}"])
    return d

def prep_l0a1a(core, inp):
    d = prep_l0a(core, inp)
    return {k: d[k] for k in ("xT", "wqkvT", "wzT", "wbT", "waT")}

def prep_l0a1b(core, inp, p1ares):
    d = prep_l0a(core, inp)
    out = {k: d[k] for k in ("convw", "convb", "alog", "dtb",
                             "ws_l1w1", "ws_l1w2", "ws_l2w1", "ws_l2w2",
                             "ws_l3w1", "ws_l3w2")}
    out["pre_d"] = np.asarray(p1ares["pre_d"])
    out["ba_d"] = np.asarray(p1ares["ba_d"])
    return out

def prep_l0a(core, inp, T=None):
    bb, j = core // 2, core % 2
    g = lambda n: np.asarray(inp[n], np.float32)
    wq = g("L0_attn_w_qkvz")
    qs = wq[:, j*1024:(j+1)*1024]
    ks = wq[:, 2048 + j*1024:2048 + (j+1)*1024]
    vs = wq[:, 4096 + j*2048:4096 + (j+1)*2048]
    zs = wq[:, 8192 + j*2048:8192 + (j+1)*2048]
    wba = g("L0_attn_w_ba")
    cwf = g("L0_attn_conv_w")
    cbf = g("L0_attn_conv_b")
    qsl = slice(j*1024, (j+1)*1024)
    ksl = slice(2048 + j*1024, 2048 + (j+1)*1024)
    vsl = slice(4096 + j*2048, 4096 + (j+1)*2048)
    mycw = np.concatenate([cwf[:, qsl], cwf[:, ksl], cwf[:, vsl]], 1)
    mycb = np.concatenate([cbf[qsl], cbf[ksl], cbf[vsl]])
    cw = np.zeros((128, 128), np.float32)
    cb = np.zeros((128, 32), np.float32)
    for c in range(32):
        for jt in range(4):
            cw[:, c*4+jt] = mycw[jt, c*128:(c+1)*128]
        cb[:, c] = mycb[c*128:(c+1)*128]
    blkS = np.zeros((128, 128), np.float32)
    blkU = np.zeros((128, 128), np.float32)
    for o in (0, 64):
        blkS[o:o+64, o:o+64] = np.tril(np.full((64, 64), -1.0), -1)
        blkU[o:o+64, o:o+64] = np.triu(np.ones((64, 64)), 0)
    hsl = slice(j*16, (j+1)*16)
    def shard(wname):
        wfull = np.asarray(inp[wname], np.float32)
        per = wfull.shape[0] // 8
        return to_f32(wfull[core*per:(core+1)*per])
    return {
        "xT": to_f32(np.asarray(inp["L0_attn_hidden_states"][bb]).T),
        "wqkvT": to_f32(np.concatenate([qs, ks, vs], 1)),
        "wzT": to_f32(zs),
        "wbT": to_f32(wba[:, hsl]),
        "waT": to_f32(wba[:, 32 + j*16:32 + (j+1)*16]),
        "convw": cw, "convb": cb,
        "alog": to_f32(g("L0_attn_A_log")[hsl].reshape(16, 1)),
        "dtb": to_f32(g("L0_attn_dt_bias")[hsl].reshape(16, 1)),
        "nwbc": to_bf(np.broadcast_to(g("L0_attn_norm_weight"), (128, 128))),
        "identD": np.eye(128, dtype=ml_dtypes.bfloat16),
        "identDf": np.eye(128, dtype=np.float32),
        "mtrilSw": to_bf(blkS),
        "mtriuIw": to_bf(blkU),
        "ws_l1w1": shard("L1_ffn_w1"), "ws_l1w2": shard("L1_ffn_w2"),
        "ws_l2w1": shard("L2_ffn_w1"), "ws_l2w2": shard("L2_ffn_w2"),
        "ws_l3w1": shard("L3_ffn_W1"), "ws_l3w2": shard("L3_ffn_W2"),
    }


_CORES = list(range(8))
_progs = {}

def _prog(name, bld):
    if name not in _progs:
        _progs[name] = bld().finalize()
    return _progs[name]

def _run(name, bld, preps):
    return run_bass_kernel_spmd(_prog(name, bld), preps, core_ids=_CORES)

def prep_q_dev(core, inp, stats):
    def shard(wname):
        w = np.asarray(inp[wname], np.float32)
        per = w.shape[0] // 8
        return to_f32(w[core*per:(core+1)*per])
    d = {"l0": shard("L0_ffn_weight"),
         "l1w1": shard("L1_ffn_w1"), "l1w2": shard("L1_ffn_w2"),
         "l2w1": shard("L2_ffn_w1"), "l2w2": shard("L2_ffn_w2"),
         "l3w1": shard("L3_ffn_W1"), "l3w2": shard("L3_ffn_W2")}
    S = np.stack(stats)
    for i, snm in enumerate(["s_l1w1", "s_l1w2", "s_l2w1", "s_l2w2"]):
        d[snm] = np.ascontiguousarray(S[:, i][None, :])
    d["s_l3w1"] = np.ascontiguousarray(S[:, 6][None, :])
    d["s_l3w2"] = np.ascontiguousarray(S[:, 7][None, :])
    return d

def _xasm(r):
    x = np.zeros((4, 2048, E), np.float32)
    for c in range(8):
        bb, j = c // 2, c % 2
        x[bb, j*1024:(j+1)*1024] = np.asarray(r.results[c]["xout"])
    return x

def _aoasm(r):
    ao = [np.zeros((E, 2048), ml_dtypes.bfloat16) for _ in range(4)]
    for c in range(8):
        bb, j = c // 2, c % 2
        ao[bb][j*1024:(j+1)*1024, :] = np.asarray(r.results[c]["ao_out"])
    return ao

def run_chain(inp):
    r1a = _run("l0a1a", build_l0a_p1a, [prep_l0a1a(c, inp) for c in range(8)])
    r1b = _run("l0a1b", build_l0a_p1b,
               [prep_l0a1b(c, inp, r1a.results[c]) for c in range(8)])
    stats = [np.asarray(r1b.results[c]["stat_out"], np.float32)[0] for c in range(8)]
    merged = [{"z_d": r1a.results[c]["z_d"],
               "conv_d": r1b.results[c]["conv_d"],
               "ssc_d": r1b.results[c]["ssc_d"],
               **{f"tab_{nm}": r1b.results[c][f"tab_{nm}"] for nm in L0A_TABS}}
              for c in range(8)]
    ghalves = {}
    for kk0 in (0, 4):
        r2 = _run(f"l0a2_{kk0}", (lambda kk0=kk0: build_l0a_p2(kk0)),
                  [prep_l0a_p2(c, inp, merged[c]) for c in range(8)])
        for c in range(8):
            ghalves[(c, kk0)] = np.asarray(r2.results[c]["gated_out"])
    gfull = [np.concatenate([ghalves[(2*bb, 0)], ghalves[(2*bb, 4)],
                             ghalves[(2*bb+1, 0)], ghalves[(2*bb+1, 4)]], 1)
             for bb in range(4)]
    rq = _run("q", build_q, [prep_q_dev(c, inp, stats) for c in range(8)])
    wq = {nm: np.concatenate([np.asarray(rq.results[c][nm + "_q"])
                              for c in range(8)], 0)
          for nm in ("l0", "l1w1", "l1w2", "l2w1", "l2w2", "l3w1", "l3w2")}
    wq["am3"] = np.asarray(rq.results[0]["am3"], np.float32)
    gam = np.concatenate([np.asarray(rq.results[c]["l0_g"], np.float32)
                          for c in range(8)], 0)
    r = _run("l0b", build_l0b,
             [prep_l0b(c, inp, gated=gfull, wq0=wq["l0"], gam=gam)
              for c in range(8)])
    x2 = _xasm(r)
    r = _run("l1a", build_l1a, [prep_l1a(c, inp, x2=x2) for c in range(8)])
    o_h = {(c // 2, c % 2): np.asarray(r.results[c]["o_out"]) for c in range(8)}
    sq_h = {(c // 2, c % 2): np.asarray(r.results[c]["ssq_out"], np.float32)
            for c in range(8)}
    r = _run("l1b", build_l1b,
             [prep_l1b(c, inp, wq=wq, x2=x2, o_h=o_h, sq_h=sq_h)
              for c in range(8)])
    x4 = _xasm(r)
    r = _run("l2a", build_l2a, [prep_l2a(c, inp, x4=x4) for c in range(8)])
    ao2 = _aoasm(r)
    r = _run("l2b", build_l2b,
             [prep_l2b(c, inp, ao_full=ao2, wq=wq, x4=x4) for c in range(8)])
    x6 = _xasm(r)
    r = _run("l3a", build_l3a, [prep_l3a(c, inp, x6=x6) for c in range(8)])
    ao3 = _aoasm(r)
    r = _run("l3b", build_l3b,
             [prep_l3b(c, inp, ao_full=ao3, wq=wq, x6=x6) for c in range(8)])
    return _xasm(r)



for _k, _v in list(globals().items()):
    if not _k.startswith("_"):
        setattr(KO, _k, _v)

def kernel(**inputs):
    return run_chain(inputs)
